# version 31
# baseline (speedup 1.0000x reference)
import numpy as np
import ml_dtypes
import concourse.bass as bass
import concourse.mybir as mybir
from concourse.bass_utils import run_bass_kernel_spmd

F32 = mybir.dt.float32
BF16 = mybir.dt.bfloat16
AF = mybir.ActivationFunctionType
ALU = mybir.AluOpType

T = 2048
E = 1024
H = 16
DH = 64
CH = 128
NCH = T // CH
NPAIR = H // 2
NTILE = T // 128
EPS = 1e-5

BF = np.dtype(ml_dtypes.bfloat16)

DEBUG_TAPS = []
STOP_STAGE = 4



STREAMS = ("sync", "gpsimd", "vector", "scalar", "tensor")


class Buf:
    __slots__ = ("w", "r")

    def __init__(self):
        self.w = []
        self.r = []


class Prog:
    def __init__(self, nc):
        self.nc = nc
        self.ops = {s: [] for s in STREAMS}
        names = (["pe", "act", "dve", "pool", "sqc", "pcp"]
                 + [f"sq{i}" for i in range(4)] + [f"gq{i}" for i in range(4)]
                 + [f"xq{i}" for i in range(10)] + [f"lw{i}" for i in range(8)])
        self.sems = {n: [nc.alloc_semaphore("s_" + n), 0] for n in names}
        self.waited = {s: {n: 0 for n in names} for s in STREAMS}
        self.sq_rr = 0
        self.gq_rr = 0

    def _emit(self, stream, sem, step, fn, reads=(), writes=(), extra=()):
        deps = list(extra)
        for b in reads:
            deps += b.w
        for b in writes:
            deps += b.w + b.r
        seen = {}
        for (sn, sv) in deps:
            if sv > seen.get(sn, 0):
                seen[sn] = sv
        waits = []
        for sn, sv in seen.items():
            if stream == "tensor" and sn == "pe":
                continue
            if self.waited[stream][sn] < sv:
                waits.append((sn, sv))
                self.waited[stream][sn] = sv
        rec = self.sems[sem]
        if step:
            rec[1] += step
        tok = (sem, rec[1])
        self.ops[stream].append((fn, waits, sem, step))
        for b in reads:
            b.r.append(tok)
        for b in writes:
            b.w = [tok]
            b.r = []
        return tok

    def pe(self, fn, reads=(), writes=(), extra=()):
        return self._emit("tensor", "pe", 1, fn, reads, writes, extra)

    def act(self, fn, reads=(), writes=(), extra=()):
        return self._emit("scalar", "act", 1, fn, reads, writes, extra)

    def dve(self, fn, reads=(), writes=(), extra=()):
        return self._emit("vector", "dve", 1, fn, reads, writes, extra)

    def pool(self, fn, reads=(), writes=(), extra=()):
        return self._emit("gpsimd", "pool", 1, fn, reads, writes, extra)

    def dma_in(self, fn, reads=(), writes=(), extra=(), const=False, sem=None):
        if sem is None:
            if const:
                sem = "sqc"
            else:
                sem = f"sq{self.sq_rr % 4}"
                self.sq_rr += 1
        return self._emit("sync", sem, 16, fn, reads, writes, extra)

    def dma_vec(self, fn, sub, reads=(), writes=(), extra=()):
        return self._emit("scalar", f"xq{sub}", 16, fn, reads, writes, extra)

    def dma_pcp(self, fn, reads=(), writes=(), extra=()):
        return self._emit("gpsimd", "pcp", 16, fn, reads, writes, extra)

    def dma_out(self, fn, reads=(), writes=(), extra=()):
        sem = f"gq{self.gq_rr % 4}"
        self.gq_rr += 1
        return self._emit("gpsimd", sem, 16, fn, reads, writes, extra)

    def final_wait(self, tokens):
        self._emit("gpsimd", "pool", 0, None, extra=tokens)

    def replay(self):
        with self.nc.Block() as block:
            def runner(stream):
                def _run(eng):
                    for fn, waits, sem, step in self.ops[stream]:
                        for (sn, sv) in waits:
                            eng.wait_ge(self.sems[sn][0], sv)
                        if fn is not None:
                            ins = fn(eng)
                            if step:
                                ins.then_inc(self.sems[sem][0], step)
                return _run
            block.sync(runner("sync"))
            block.gpsimd(runner("gpsimd"))
            block.vector(runner("vector"))
            block.scalar(runner("scalar"))
            block.tensor(runner("tensor"))


def A(ten, row0, col0, dims, rowcount=None):
    stride = ten.shape[1]
    if rowcount is None:
        rowcount = ten.shape[0] - row0
    return bass.AP(ten, row0 * stride + col0, [[stride, rowcount]] + list(dims))



def _build_slopes(n):
    def po2(m):
        start = 2.0 ** (-(2.0 ** (-(np.log2(m) - 3))))
        return [start * start ** i for i in range(m)]
    if np.log2(n).is_integer():
        return po2(n)
    closest = 2 ** int(np.floor(np.log2(n)))
    return po2(closest) + _build_slopes(2 * closest)[0::2][: n - closest]


def _consts():
    s = np.array(_build_slopes(H), dtype=np.float32)
    s = (s * np.float32(1.0 + 1e-5)).astype(np.float64)
    j = np.arange(CH, dtype=np.float64)
    diff = j[None, :, None] * 0 + (j[None, None, :] - j[None, :, None])
    m = np.where(diff >= 0, np.exp(-s[:, None, None] * diff), 0.0)
    maskt = np.transpose(m, (1, 0, 2)).reshape(CH, H * CH).astype(BF)
    lamq = np.exp(-s[:, None] * (j + 1.0)[None, :])
    lamqb = np.zeros((128, NPAIR * CH))
    for p in range(NPAIR):
        lamqb[0:64, p * CH:(p + 1) * CH] = lamq[2 * p][None, :]
        lamqb[64:128, p * CH:(p + 1) * CH] = lamq[2 * p + 1][None, :]
    lamk = np.exp(-s[:, None] * (CH - 1.0 - j)[None, :])
    lamkb = np.zeros((128, E))
    for h in range(H):
        lamkb[:, h * DH:(h + 1) * DH] = lamk[h][:, None]
    lamc = np.exp(-s * CH)
    lamcb = np.zeros((128, NPAIR * CH))
    for p in range(NPAIR):
        lamcb[0:64, p * CH:(p + 1) * CH] = lamc[2 * p]
        lamcb[64:128, p * CH:(p + 1) * CH] = lamc[2 * p + 1]
    return dict(
        maskt=maskt,
        lamqb=lamqb.astype(BF), lamkb=lamkb.astype(BF), lamcb=lamcb.astype(BF),
        identb=np.eye(128, dtype=BF),
        onesb=np.ones((128, 128), dtype=BF),
        onesf=np.ones((1, 128), dtype=np.float32))



def build(debug_taps=(), stage=4):
    nc = bass.Bass("TRN2", target_bir_lowering=False)

    dx = nc.dram_tensor("x", [T, E], F32, kind="ExternalInput").ap()
    dwq = nc.dram_tensor("wq", [E, E], F32, kind="ExternalInput").ap()
    dwk = nc.dram_tensor("wk", [E, E], F32, kind="ExternalInput").ap()
    dwv = nc.dram_tensor("wv", [E, E], F32, kind="ExternalInput").ap()
    dwo = nc.dram_tensor("wo", [E, E], F32, kind="ExternalInput").ap()
    dw = nc.dram_tensor("w", [E, E], F32, kind="ExternalInput").ap()
    dwg1h = nc.dram_tensor("wg1", [E, DH], F32, kind="ExternalInput")
    dwg2 = nc.dram_tensor("wg2", [DH, E], F32, kind="ExternalInput").ap()
    dgamb = nc.dram_tensor("gamb", [128, 1024], BF16, kind="ExternalInput").ap()
    dbetb = nc.dram_tensor("betb", [128, 1024], BF16, kind="ExternalInput").ap()
    dbv = nc.dram_tensor("bvec", [1, E], F32, kind="ExternalInput").ap()
    dldown = nc.dram_tensor("ldown", [4, 100], F32, kind="ExternalInput").ap()
    dlup = nc.dram_tensor("lup", [50, 4], F32, kind="ExternalInput").ap()
    ddaux = nc.dram_tensor("daux", [100, E], F32, kind="ExternalInput").ap()
    duauxh = nc.dram_tensor("uaux", [E, 50], F32, kind="ExternalInput")
    dmaskt = nc.dram_tensor("maskt", [128, H * CH], BF16, kind="ExternalInput").ap()
    dlamqb = nc.dram_tensor("lamqb", [128, NPAIR * CH], BF16, kind="ExternalInput").ap()
    dlamkb = nc.dram_tensor("lamkb", [128, E], BF16, kind="ExternalInput").ap()
    dlamcb = nc.dram_tensor("lamcb", [128, NPAIR * CH], BF16, kind="ExternalInput").ap()
    didb = nc.dram_tensor("identb", [128, 128], BF16, kind="ExternalInput").ap()
    donesb = nc.dram_tensor("onesb", [128, 128], BF16, kind="ExternalInput").ap()
    donesf = nc.dram_tensor("onesf", [1, 128], F32, kind="ExternalInput").ap()
    dy = nc.dram_tensor("y", [T, E], F32, kind="ExternalOutput").ap()
    dtaps = {}
    for tname in debug_taps:
        shp, dt = {"xnt": ([128, 16384], BF16), "qt": ([128, 16384], BF16),
                   "kt": ([128, 16384], BF16), "vo": ([128, 16384], BF16),
                   "ot": ([128, 16384], BF16),
                   "t1": ([128, 2048], BF16), "g": ([128, 16384], BF16),
                   "gate": ([128, 16384], BF16), "weff": ([128, 8192], BF16),
                   "kv1": ([128, 1024], BF16), "qh1": ([128, 1024], BF16),
                   "ws0": ([128, 8192], BF16),
                   "ktm0": ([128, 1024], BF16),
                   }[tname]
        dtaps[tname] = nc.dram_tensor("tap_" + tname, shp, dt,
                                      kind="ExternalOutput").ap()

    off = {}
    cur = 16768

    def place(name, nbytes, align=64):
        nonlocal cur
        cur = (cur + align - 1) // align * align
        off[name] = cur
        cur += nbytes

    place("XNT", 32768)
    place("QT", 32768)
    place("KT", 32768)
    place("VO", 32768)
    place("WS0", 16384)
    place("WS1", 16384)
    place("LAND", 4096)
    place("XNTM", 4096)
    place("KTM", 4096)
    place("QH", 4096)
    place("SAB", 4096)
    place("KVBD", 4096)
    place("MASKT", 4096)
    place("LAMQB", 2048)
    place("LAMKB", 2048)
    place("LAMCB", 2048)
    place("T1", 4096)
    place("WG1", 1024)
    place("WG2", 2048)
    place("IDB", 256)
    place("ONESB", 256)
    place("ONESF", 512)
    place("GBP", 64)
    place("SMALL", 3072)
    place("LSM", 1024)
    assert cur <= 229344, f"sbuf over budget: {cur}"

    mk = nc.alloc_sbuf_tensor_at

    XNT = mk("XNT", [128, 16384], BF16, offset=off["XNT"])
    QT = mk("QT", [128, 16384], BF16, offset=off["QT"])
    GATE = mk("GATE", [128, 16384], BF16, offset=off["QT"])
    KT = mk("KT", [128, 16384], BF16, offset=off["KT"])
    G = mk("G", [128, 16384], BF16, offset=off["KT"])
    VO = mk("VO", [128, 16384], BF16, offset=off["VO"])
    WS0 = mk("WS0", [128, 8192], BF16, offset=off["WS0"])
    WS1 = mk("WS1", [128, 8192], BF16, offset=off["WS1"])
    LAND = mk("LAND", [128, 1024], F32, offset=off["LAND"])
    XNTM = mk("XNTM", [128, 2048], BF16, offset=off["XNTM"])
    XV2 = mk("XV2", [128, 2048], F32, offset=off["VO"])
    XV3 = mk("XV3", [128, 6144], F32, offset=off["VO"] + 8192)
    WGL1 = mk("WGL1", [128, 512], F32, offset=off["XNTM"])
    WGL2 = mk("WGL2", [64, 1024], F32, offset=off["XNTM"])
    KTM = mk("KTM", [128, 2048], BF16, offset=off["KTM"])
    BVB = mk("BVB", [1, 1024], BF16, offset=off["KTM"])
    B128 = mk("B128", [128, 1024], BF16, offset=off["KTM"] + 2048)
    RBI = mk("RBI", [128, 128], F32, offset=off["KTM"] + 2048)
    QH = mk("QH", [128, 2048], BF16, offset=off["QH"])
    SGT = mk("SGT", [128, 1024], F32, offset=off["QH"])
    SGT2 = mk("SGT2", [128, 1024], F32, offset=off["SAB"])
    XTS = mk("XTS", [128, 2048], F32, offset=off["SAB"])
    SAB = mk("SAB", [128, 2048], BF16, offset=off["SAB"])
    AB_ = mk("AB_", [4, 1024], BF16, offset=off["SAB"])
    BB_ = mk("BB_", [100, 1024], BF16, offset=off["SAB"] + 2048)
    KVBD = mk("KVBD", [128, 2048], BF16, offset=off["KVBD"])
    DAUXB = mk("DAUXB", [100, 1024], BF16, offset=off["KVBD"])
    UAT = mk("UAT", [50, 1024], BF16, offset=off["KVBD"] + 2048)
    MASKT = mk("MASKT", [128, 2048], BF16, offset=off["MASKT"])
    LANDB = mk("LANDB", [128, 1024], F32, offset=off["MASKT"])
    BETB = mk("BETB", [128, 1024], BF16, offset=off["MASKT"])
    CC2e = mk("CC2e", [128, 1024], BF16, offset=off["MASKT"] + 2048)
    LAMQB = mk("LAMQB", [128, 1024], BF16, offset=off["LAMQB"])
    CC = mk("CC", [128, 1024], BF16, offset=off["LAMQB"])
    LAMKB = mk("LAMKB", [128, 1024], BF16, offset=off["LAMKB"])
    CCo = mk("CCo", [128, 1024], BF16, offset=off["LAMKB"])
    CC2o = mk("CC2o", [128, 1024], BF16, offset=off["KTM"])
    LAMCB = mk("LAMCB", [128, 1024], BF16, offset=off["LAMCB"])
    GAMB = mk("GAMB", [128, 1024], BF16, offset=off["LAMCB"])
    T1 = mk("T1", [128, 2048], BF16, offset=off["T1"])
    WG1B = mk("WG1B", [128, 512], BF16, offset=off["WG1"])
    WG2B = mk("WG2B", [64, 1024], BF16, offset=off["WG2"])
    IDB = mk("IDB", [128, 128], BF16, offset=off["IDB"])
    ONESB = mk("ONESB", [128, 128], BF16, offset=off["ONESB"])
    ONESF = mk("ONESF", [1, 128], F32, offset=off["ONESF"])
    SUMS = mk("SUMS", [128, 16], F32, offset=off["SMALL"])
    SUMSI = mk("SUMSI", [128, 16], mybir.dt.int32, offset=off["SMALL"])
    MROW = mk("MROW", [1, 128], F32, offset=off["SMALL"] + 64)
    VROW = mk("VROW", [1, 128], F32, offset=off["SMALL"] + 576)
    RSROW = mk("RSROW", [1, 128], F32, offset=off["SMALL"] + 1088)
    LUPF = mk("LUPF", [50, 4], F32, offset=off["SMALL"] + 1600)
    LDOWNF = mk("LDOWNF", [4, 100], F32, offset=off["SMALL"] + 1632)
    LSM1 = mk("LSM1", [50, 4], BF16, offset=off["LSM"])
    LSM2 = mk("LSM2", [4, 100], BF16, offset=off["LSM"] + 32)
    LSM3 = mk("LSM3", [128, 400], BF16, offset=off["SMALL"] + 2048)
    EPSC = mk("EPSC", [128, 1], F32, offset=off["SMALL"] + 2848)
    BNS = mk("BNS", [128, 24], F32, offset=off["SMALL"] + 2880)
    nc.const_aps.register(F32, EPS, EPSC[:])

    PB = [nc.alloc_psum_tensor(f"pb{i}", [128, 512], F32) for i in range(8)]

    def pbh(i):
        return PB[i][:].bitcast(BF16)

    p = Prog(nc)

    bXNTM = [[Buf(), Buf()], [Buf(), Buf()]]
    bSUM = [Buf(), Buf()]
    bXNT = [[Buf() for _ in range(NTILE)] for _ in range(8)]
    bPB = [Buf() for _ in range(8)]
    bLAND = Buf()
    bWS0 = [Buf() for _ in range(8)]
    bWS1 = [Buf() for _ in range(8)]
    bQT = [[Buf() for _ in range(4)] for _ in range(8)]
    bKT = [[Buf() for _ in range(4)] for _ in range(8)]
    bVO = [[Buf(), Buf()] for _ in range(NCH)]
    bT1 = [Buf() for _ in range(4)]
    bKTM = [Buf(), Buf()]
    bQH = [Buf(), Buf()]
    bSAB = [Buf(), Buf()]
    bKV = [Buf(), Buf()]
    bG = [Buf() for _ in range(NCH)]
    bGATE = [[Buf() for _ in range(4)] for _ in range(8)]
    bWG1 = Buf(); bWG2 = Buf()
    bMASK = Buf(); bLAMQ = Buf(); bLAMK = Buf(); bLAMC = Buf()
    bIDB = Buf(); bONESB = Buf(); bONESF = Buf()
    bLSM = Buf()
    bBV = Buf(); bB128 = Buf()
    bDAUX = bKV[0]; bUAT = bKV[1]
    bABa = bSAB[0]; bABb = bSAB[1]
    bXV = [Buf() for _ in range(8)]
    bROW = Buf()
    bCC = Buf(); bCC2 = Buf(); bSG2 = Buf()
    bSG = Buf()

    for dstap, src, b in ((LAMQB[:], dlamqb, bLAMQ),
                          (LAMKB[:], dlamkb, bLAMK), (LAMCB[:], dlamcb, bLAMC),
                          (IDB[:], didb, bIDB),
                          (ONESB[:], donesb, bONESB), (ONESF[:], donesf, bONESF),
                          (LUPF[:], dlup, bLSM), (LDOWNF[:], dldown, bLSM)):
        p.dma_in(lambda e, d=dstap, s=src: e.dma_start(out=d, in_=s),
                 writes=[b], const=True)
    _ctot = p.sems["sqc"][1]
    for _b in (bLAMQ, bLAMK, bLAMC, bIDB, bONESB, bONESF, bLSM):
        _b.w = [("sqc", _ctot)]
    bEPS = Buf()
    p.dve(lambda e: e.memset(EPSC[:], EPS), writes=[bEPS])

    bLANDB = Buf()

    class WStream:
        def __init__(self):
            self.chunks = []
            self.nd = 0
            self.nc = 0
            self.extra = {}

        def add(self, dsrc, dten, bufs, extra=()):
            for k in range(8):
                if extra:
                    self.extra[len(self.chunks)] = list(extra)
                self.chunks.append((dsrc, k, dten, bufs))

        def _dma(self):
            i = self.nd
            if i >= len(self.chunks):
                return
            dsrc, k, _, _ = self.chunks[i]
            land, lb = (LAND, bLAND) if i % 2 == 0 else (LANDB, bLANDB)
            p.dma_in(lambda e, k=k, land=land, dsrc=dsrc: e.dma_start(
                out=land[:], in_=dsrc[k * 128:(k + 1) * 128, :]),
                writes=[lb] + self.extra.get(i, []))
            self.nd += 1

        def prime(self):
            while self.nd < min(self.nc + 2, len(self.chunks)):
                self._dma()

        def cast(self):
            i = self.nc
            _, k, dten, bufs = self.chunks[i]
            land, lb = (LAND, bLAND) if i % 2 == 0 else (LANDB, bLANDB)
            p.dve(lambda e, k=k, dten=dten, land=land: e.tensor_copy(
                out=dten[:, k * 1024:(k + 1) * 1024], in_=land[:]),
                reads=[lb], writes=[bufs[k]])
            self.nc += 1
            self._dma()

    ws = WStream()
    ws.add(dwq, WS0, bWS0)
    ws.add(dwk, WS1, bWS1)
    ws.add(dwv, WS0, bWS0)
    ws.add(dw, WS1, bWS1)
    ws.prime()

    def xslot(t):
        s = t if t < 10 else t - 10
        if s == 0:
            return XTS[:, 0:1024], [bSAB[0], bSAB[1]]
        if s == 1:
            return XTS[:, 1024:2048], [bKV[0], bKV[1]]
        if s < 4:
            return XV2[:, (s - 2) * 1024:(s - 1) * 1024], [bXV[s - 2]]
        return XV3[:, (s - 4) * 1024:(s - 3) * 1024], [bXV[s - 2]]

    def emit_xdma(t):
        xt, bb = xslot(t)
        p.dma_vec(lambda e, t=t, xt=xt: e.dma_start(
            out=xt, in_=dx[t * 128:(t + 1) * 128, :]),
            t if t < 10 else t - 10, writes=bb)

    def proj_fmajor_t4(wten, wbufs, dstten, dstbufs, act_fn, t4):
        for oc in range(8):
            bank = 4 + (t4 * 8 + oc) % 4
            for k in range(8):
                need = [bXNT[k][t4 * 4 + i] for i in range(4)]
                p.pe(lambda e, k=k, oc=oc, t4=t4, bank=bank, wten=wten: e.matmul(
                    PB[bank][:],
                    wten[:, k * 1024 + oc * 128:k * 1024 + (oc + 1) * 128],
                    XNT[:, k * 2048 + t4 * 512:k * 2048 + (t4 + 1) * 512],
                    start=(k == 0), stop=(k == 7)),
                    reads=[wbufs[k]] + need, writes=[bPB[bank]])
            p.act(lambda e, oc=oc, t4=t4, bank=bank, dstten=dstten: e.activation(
                out=dstten[:, oc * 2048 + t4 * 512:oc * 2048 + (t4 + 1) * 512],
                in_=PB[bank][:], func=act_fn),
                reads=[bPB[bank]], writes=[dstbufs[oc][t4]])

    def p1_stats(t):
        sl = t % 2
        xt, xb = xslot(t)
        m = SUMS[:, 2 * sl:2 * sl + 1]
        va = SUMS[:, 2 * sl + 1:2 * sl + 2]
        sd = SUMS[:, 8 + sl:9 + sl]
        yv = SUMS[:, 10 + sl:11 + sl]
        for h in range(2):
            p.dve(lambda e, xt=xt, sl=sl, h=h: e.bn_stats(
                out=BNS[:, sl * 12 + h * 6:sl * 12 + h * 6 + 6],
                in_=xt[:, h * 512:(h + 1) * 512]),
                reads=xb, writes=[bSUM[sl]])
        p.dve(lambda e, sl=sl: e.bn_aggr(
            out=SUMS[:, 2 * sl:2 * sl + 2],
            in_=BNS[:, sl * 12:sl * 12 + 12]),
            reads=[bSUM[sl]], writes=[bSUM[sl]])
        p.act(lambda e, va=va, sd=sd: e.activation(
            out=sd, in_=va, func=AF.Sqrt, bias=EPS),
            reads=[bSUM[sl], bEPS], writes=[bSUM[sl]])
        p.dve(lambda e, sd=sd, yv=yv: e.reciprocal(out=yv, in_=sd),
              reads=[bSUM[sl]], writes=[bSUM[sl]])
        xn = XNTM[:, sl * 1024:(sl + 1) * 1024]
        for h in range(2):
            p.dve(lambda e, xt=xt, m=m, yv=yv, xn=xn, h=h: e.tensor_scalar(
                out=xn[:, h * 512:(h + 1) * 512], in0=xt[:, h * 512:(h + 1) * 512],
                scalar1=m, scalar2=yv, op0=ALU.subtract, op1=ALU.mult),
                reads=xb + [bSUM[sl]], writes=[bXNTM[sl][h]])
        if t + 10 < NTILE:
            emit_xdma(t + 10)

    def p1_transpose(t):
        sl = t % 2
        xn = XNTM[:, sl * 1024:(sl + 1) * 1024]
        bank = t % 4
        for f in range(8):
            p.pe(lambda e, f=f, xn=xn, bank=bank: e.matmul(
                pbh(bank)[:, f * 128:(f + 1) * 128],
                xn[:, f * 128:(f + 1) * 128], IDB[:],
                start=(f == 0), stop=(f == 7), is_transpose=True),
                reads=[bXNTM[sl][f // 4], bIDB], writes=[bPB[bank]])
        for hf in range(2):
            dst = A(XNT, 0, hf * 4 * 2048 + t * 128, [[2048, 4], [1, 128]])
            p.act(lambda e, bank=bank, dst=dst, hf=hf: e.activation(
                out=dst, in_=pbh(bank)[:, hf * 512:(hf + 1) * 512], func=AF.Copy),
                reads=[bPB[bank]],
                writes=[bXNT[f][t] for f in range(hf * 4, hf * 4 + 4)])

    bDY = Buf()
    for _t in range(10):
        emit_xdma(_t)
    p1_stats(0)
    ws.cast(); ws.cast()
    for t in range(1, NTILE):
        p1_stats(t)
        if t < 12:
            ws.cast()
        if t < 4:
            ws.cast()
        p1_transpose(t - 1)
        if (t - 1) % 4 == 3 and t - 1 < 12:
            proj_fmajor_t4(WS0, bWS0, QT, bQT, AF.Silu, (t - 1) // 4)
        if t - 1 >= 12:
            proj_fmajor_t4(WS1, bWS1, KT, bKT, AF.Silu, t - 13)
    p1_transpose(NTILE - 1)
    proj_fmajor_t4(WS0, bWS0, QT, bQT, AF.Silu, 3)
    proj_fmajor_t4(WS1, bWS1, KT, bKT, AF.Silu, 3)

    for _ in range(8):
        ws.cast()

    srcg1 = bass.AP(dwg1h, 0, [[64, 128], [8192, 8], [1, 64]])
    p.dma_in(lambda e, s=srcg1: e.dma_start(out=WGL1[:], in_=s),
             writes=bXNTM[0] + bXNTM[1], sem='lw0')
    p.pool(lambda e: e.tensor_copy(out=WG1B[:], in_=WGL1[:]),
           reads=[bXNTM[0][0]], writes=[bWG1])
    p.dma_in(lambda e: e.dma_start(out=WGL2[:], in_=dwg2),
             writes=bXNTM[0] + bXNTM[1], sem='lw1')
    p.pool(lambda e: e.tensor_copy(out=WG2B[:], in_=WGL2[:]),
           reads=[bXNTM[0][0]], writes=[bWG2])

    for _ in range(8):
        ws.cast()
    for tc in range(NTILE):
        for h2 in range(2):
            bank = 4 + (tc * 2 + h2) % 4
            for k in range(8):
                p.pe(lambda e, tc=tc, h2=h2, k=k, bank=bank: e.matmul(
                    PB[bank][:],
                    XNT[:, k * 2048 + tc * 128:k * 2048 + (tc + 1) * 128],
                    WS0[:, k * 1024 + h2 * 512:k * 1024 + (h2 + 1) * 512],
                    start=(k == 0), stop=(k == 7)),
                    reads=[bWS0[k], bXNT[k][tc]], writes=[bPB[bank]])
            xw = [bXV[tc // 2]]
            p.act(lambda e, tc=tc, h2=h2, bank=bank: e.activation(
                out=VO[:, tc * 1024 + h2 * 512:tc * 1024 + (h2 + 1) * 512],
                in_=PB[bank][:], func=AF.Silu),
                reads=[bPB[bank]], writes=[bVO[tc][h2]] + xw)

    for t4 in range(4):
        bank = 4 + t4 % 4
        for k in range(8):
            need = [bXNT[k][t4 * 4 + i] for i in range(4)]
            p.pe(lambda e, k=k, t4=t4, bank=bank: e.matmul(
                PB[bank][0:64, :],
                WG1B[:, k * 64:(k + 1) * 64],
                XNT[:, k * 2048 + t4 * 512:k * 2048 + (t4 + 1) * 512],
                start=(k == 0), stop=(k == 7)),
                reads=[bWG1] + need, writes=[bPB[bank]])
        p.act(lambda e, t4=t4, bank=bank: e.activation(
            out=T1[0:64, t4 * 512:(t4 + 1) * 512], in_=PB[bank][0:64, :],
            func=AF.Copy), reads=[bPB[bank]], writes=[bT1[t4]])

    tap_toks = []
    if "xnt" in dtaps:
        tap_toks.append(p.dma_out(lambda e: e.dma_start(out=dtaps["xnt"], in_=XNT[:]),
                  reads=[bXNT[f][t] for f in range(8) for t in range(NTILE)]))
    if "qt" in dtaps:
        tap_toks.append(p.dma_out(lambda e: e.dma_start(out=dtaps["qt"], in_=QT[:]),
                  reads=[bQT[oc][t] for oc in range(8) for t in range(4)]))
    if "kt" in dtaps:
        tap_toks.append(p.dma_out(lambda e: e.dma_start(out=dtaps["kt"], in_=KT[:]),
                  reads=[bKT[oc][t] for oc in range(8) for t in range(4)]))
    if "vo" in dtaps:
        tap_toks.append(p.dma_out(lambda e: e.dma_start(out=dtaps["vo"], in_=VO[:]),
                  reads=[b for c in range(NCH) for b in bVO[c]]))
    if "t1" in dtaps:
        tap_toks.append(p.dma_out(lambda e: e.dma_start(out=dtaps["t1"], in_=T1[:]),
                  reads=[bT1[t] for t in range(4)]))

    if stage < 3:
        p.final_wait(tap_toks)
        p.replay()
        return nc

    p.dma_in(lambda e: e.dma_start(out=MASKT[:], in_=dmaskt),
             writes=[bMASK, bLANDB], sem='lw2')
    p.dma_pcp(lambda e: e.dma_start(out=dy, in_=dx), writes=[bDY])

    p.dve(lambda e: e.memset(KVBD[:, 0:1024], 0.0),
          reads=(), writes=[bKV[0]])
    for c in range(NCH):
        csl = c % 2
        nsl = (c + 1) % 2
        for w in range(2):
            sa, sb = 1 + 2 * w, 2 + 2 * w
            for i in range(4):
                pp = w * 4 + i
                p.pe(lambda e, pp=pp, i=i, c=c, sa=sa: e.matmul(
                    PB[sa][:, i * 128:(i + 1) * 128],
                    A(KT, 0, pp * 2048 + c * 128, [[1, 128]], rowcount=64),
                    A(QT, 0, pp * 2048 + c * 128, [[1, 128]], rowcount=64),
                    start=(i == 0), stop=(i == 3), tile_position=(0, 0)),
                    reads=[bKT[pp][c // 4], bQT[pp][c // 4]], writes=[bPB[sa]])
                p.pe(lambda e, pp=pp, i=i, c=c, sb=sb: e.matmul(
                    PB[sb][:, i * 128:(i + 1) * 128],
                    A(KT, 64, pp * 2048 + c * 128, [[1, 128]]),
                    A(QT, 64, pp * 2048 + c * 128, [[1, 128]]),
                    start=(i == 0), stop=(i == 3), tile_position=(64, 0)),
                    reads=[bKT[pp][c // 4], bQT[pp][c // 4]], writes=[bPB[sb]])
        for w in range(2):
            sa, sb = 1 + 2 * w, 2 + 2 * w
            p.dve(lambda e, sa=sa, w=w: e.tensor_tensor(
                out=A(SAB, 0, w * 1024, [[256, 4], [1, 128]]),
                in0=A(PB[sa], 0, 0, [[128, 4], [1, 128]]),
                in1=A(MASKT, 0, w * 1024, [[256, 4], [1, 128]]),
                op=ALU.mult),
                reads=[bPB[sa], bMASK], writes=[bSAB[w]])
            p.dve(lambda e, sb=sb, w=w: e.tensor_tensor(
                out=A(SAB, 0, w * 1024 + 128, [[256, 4], [1, 128]]),
                in0=A(PB[sb], 0, 0, [[128, 4], [1, 128]]),
                in1=A(MASKT, 0, w * 1024 + 128, [[256, 4], [1, 128]]),
                op=ALU.mult),
                reads=[bPB[sb], bMASK], writes=[bSAB[w]])
        for f in range(8):
            p.pe(lambda e, c=c, f=f: e.matmul(
                pbh(0)[:, f * 128:(f + 1) * 128],
                KT[:, f * 2048 + c * 128:f * 2048 + (c + 1) * 128], IDB[:],
                start=(f == 0), stop=(f == 7), is_transpose=True),
                reads=[bKT[f][c // 4], bIDB], writes=[bPB[0]])
        p.dve(lambda e, csl=csl: e.tensor_tensor(
            out=KTM[:, csl * 1024:(csl + 1) * 1024], in0=pbh(0)[:],
            in1=LAMKB[:], op=ALU.mult),
            reads=[bPB[0], bLAMK], writes=[bKTM[csl]])
        qsrc = A(QT, 0, c * 128, [[2048, 8], [1, 128]])
        qdst = A(QH, 0, csl * 1024, [[128, 8], [1, 128]])
        lsrc = A(LAMQB, 0, 0, [[128, 8], [1, 128]])
        p.pool(lambda e, qsrc=qsrc, qdst=qdst, lsrc=lsrc: e.tensor_tensor(
            out=qdst, in0=qsrc, in1=lsrc, op=ALU.mult),
            reads=[bQT[pp][c // 4] for pp in range(8)] + [bLAMQ],
            writes=[bQH[csl]])

        for w in range(2):
            sa, sb = 1 + 2 * w, 2 + 2 * w
            for i in range(4):
                pp = w * 4 + i
                ob = 6 + i // 2
                ocol = (i % 2) * 256
                p.pe(lambda e, pp=pp, c=c, ob=ob, ocol=ocol, w=w, i=i: e.matmul(
                    PB[ob][:, ocol:ocol + 256],
                    VO[:, c * 1024 + pp * 128:c * 1024 + (pp + 1) * 128],
                    A(SAB, 0, w * 1024 + i * 256, [[1, 256]]),
                    start=((i % 2) == 0), stop=(c == 0 and (i % 2) == 1)),
                    reads=[bVO[c][w], bSAB[w]], writes=[bPB[ob]])
            if c > 0:
                for i in range(4):
                    pp = w * 4 + i
                    ob = 6 + i // 2
                    ocol = (i % 2) * 256
                    rhs = bass.AP(QH, csl * 1024 + pp * 128,
                                  [[2048, 128], [0, 2], [1, 128]])
                    p.pe(lambda e, pp=pp, ob=ob, ocol=ocol, rhs=rhs, csl=csl, i=i: e.matmul(
                        PB[ob][:, ocol:ocol + 256],
                        KVBD[:, csl * 1024 + pp * 128:csl * 1024 + (pp + 1) * 128],
                        rhs, start=False, stop=((i % 2) == 1)),
                        reads=[bKV[csl], bQH[csl]], writes=[bPB[ob]])
            for i in range(4):
                pp = w * 4 + i
                p.pe(lambda e, pp=pp, c=c, csl=csl, i=i: e.matmul(
                    PB[5][:, i * 128:(i + 1) * 128],
                    KTM[:, csl * 1024 + pp * 128:csl * 1024 + (pp + 1) * 128],
                    VO[:, c * 1024 + pp * 128:c * 1024 + (pp + 1) * 128],
                    start=(i == 0), stop=(i == 3)),
                    reads=[bKTM[csl], bVO[c][w]], writes=[bPB[5]])
            p.pool(lambda e, csl=csl, nsl=nsl, w=w: e.tensor_tensor(
                out=KVBD[:, nsl * 1024 + w * 512:nsl * 1024 + (w + 1) * 512],
                in0=KVBD[:, csl * 1024 + w * 512:csl * 1024 + (w + 1) * 512],
                in1=LAMCB[:, w * 512:(w + 1) * 512], op=ALU.mult),
                reads=[bKV[csl], bLAMC], writes=[bKV[nsl]])
            for bb_ in range(2):
                ob = 6 + bb_
                dstA = A(VO, 0, c * 1024 + (w * 4 + bb_ * 2) * 128,
                         [[128, 2], [1, 128]], rowcount=64)
                srcA = A(PB[ob], 0, 0, [[256, 2], [1, 128]], rowcount=64)
                p.act(lambda e, d=dstA, s=srcA: e.activation(
                    out=d, in_=s, func=AF.Copy),
                    reads=[bPB[ob]], writes=[bVO[c][w]])
                dstB = A(VO, 64, c * 1024 + (w * 4 + bb_ * 2) * 128,
                         [[128, 2], [1, 128]])
                srcB = A(PB[ob], 64, 128, [[256, 2], [1, 128]])
                p.act(lambda e, d=dstB, s=srcB: e.activation(
                    out=d, in_=s, func=AF.Copy),
                    reads=[bPB[ob]], writes=[bVO[c][w]])
            oA = A(KVBD, 0, nsl * 1024 + w * 512, [[128, 4], [1, 64]], rowcount=64)
            iA = A(PB[5], 0, 0, [[128, 4], [1, 64]], rowcount=64)
            p.dve(lambda e, oA=oA, iA=iA: e.tensor_tensor(
                out=oA, in0=oA, in1=iA, op=ALU.add),
                reads=[bPB[5], bKV[nsl]], writes=[bKV[nsl]])
            oB = A(KVBD, 64, nsl * 1024 + w * 512 + 64, [[128, 4], [1, 64]])
            iB = A(PB[5], 64, 64, [[128, 4], [1, 64]])
            p.dve(lambda e, oB=oB, iB=iB: e.tensor_tensor(
                out=oB, in0=oB, in1=iB, op=ALU.add),
                reads=[bPB[5], bKV[nsl]], writes=[bKV[nsl]])

        if c == 0 and "kv1" in dtaps:
            tap_toks.append(p.dma_out(
                lambda e: e.dma_start(out=dtaps["kv1"], in_=KVBD[:, 1024:2048]),
                reads=[bKV[1]]))
        if c == 0 and "ktm0" in dtaps:
            tap_toks.append(p.dma_out(
                lambda e: e.dma_start(out=dtaps["ktm0"], in_=KTM[:, 0:1024]),
                reads=[bKTM[0]]))
        if c == 1 and "qh1" in dtaps:
            tap_toks.append(p.dma_out(
                lambda e: e.dma_start(out=dtaps["qh1"], in_=QH[:, 1024:2048]),
                reads=[bQH[1]]))

    if "ot" in dtaps:
        tap_toks.append(p.dma_out(lambda e: e.dma_start(out=dtaps["ot"], in_=VO[:]),
                  reads=[b for c in range(NCH) for b in bVO[c]]))

    if stage < 4:
        p.final_wait(tap_toks)
        p.replay()
        return nc

    if "ws0" in dtaps:
        tap_toks.append(p.dma_out(lambda e: e.dma_start(
            out=dtaps["ws0"], in_=WS0[:]), reads=[bWS0[k] for k in range(8)]))
    for a in range(8):
        bank = a % 4
        for b_ in range(8):
            p.pe(lambda e, a=a, b_=b_, bank=bank: e.matmul(
                pbh(bank)[:, b_ * 128:(b_ + 1) * 128],
                WS1[:, b_ * 1024 + a * 128:b_ * 1024 + (a + 1) * 128], IDB[:],
                start=(b_ == 0), stop=(b_ == 7), is_transpose=True),
                reads=[bWS1[b_], bIDB], writes=[bPB[bank]])
        p.dve(lambda e, a=a, bank=bank: e.tensor_copy(
            out=WS0[:, a * 1024:(a + 1) * 1024], in_=pbh(bank)[:]),
            reads=[bPB[bank]], writes=[bWS0[a]])

    p.dma_in(lambda e: e.dma_start(out=LAND[0:100, :], in_=ddaux),
             writes=[bLAND], sem='lw5')
    p.pool(lambda e: e.tensor_copy(out=DAUXB[:], in_=LAND[0:100, :]),
           reads=[bLAND], writes=[bDAUX])
    srcu = bass.AP(duauxh, 0, [[50, 128], [6400, 8], [1, 50]])
    p.dma_in(lambda e, s=srcu: e.dma_start(out=LAND[:, 0:400], in_=s),
             writes=[bLAND], sem='lw6')
    p.pool(lambda e: e.tensor_copy(out=LSM3[:], in_=LAND[:, 0:400]),
           reads=[bLAND], writes=[bLSM])
    p.pool(lambda e: e.tensor_copy(out=LSM1[:], in_=LUPF[:]),
           reads=[bLSM], writes=[bLSM])
    p.pool(lambda e: e.tensor_copy(out=LSM2[:], in_=LDOWNF[:]),
           reads=[bLSM], writes=[bLSM])
    for gidx in range(8):
        p.pe(lambda e, g=gidx: e.matmul(
            pbh(4)[0:50, g * 128:(g + 1) * 128],
            LSM3[:, g * 50:(g + 1) * 50], IDB[:],
            start=(g == 0), stop=(g == 7), is_transpose=True),
            reads=[bLSM, bIDB], writes=[bPB[4]])
    p.dve(lambda e: e.tensor_copy(out=UAT[:], in_=pbh(4)[0:50, 0:1024]),
          reads=[bPB[4]], writes=[bUAT])
    for h in range(2):
        p.pe(lambda e, h=h: e.matmul(
            PB[4 + h][0:4, :], LSM1[:], UAT[:, h * 512:(h + 1) * 512],
            start=True, stop=True),
            reads=[bLSM, bUAT], writes=[bPB[4 + h]])
        p.act(lambda e, h=h: e.activation(
            out=AB_[0:4, h * 512:(h + 1) * 512], in_=PB[4 + h][0:4, :],
            func=AF.Copy), reads=[bPB[4 + h]], writes=[bABa])
    for h in range(2):
        p.pe(lambda e, h=h: e.matmul(
            PB[6 + h][0:100, :], LSM2[:], AB_[0:4, h * 512:(h + 1) * 512],
            start=True, stop=True),
            reads=[bLSM, bABa], writes=[bPB[6 + h]])
        p.act(lambda e, h=h: e.activation(
            out=BB_[0:100, h * 512:(h + 1) * 512], in_=PB[6 + h][0:100, :],
            func=AF.Copy), reads=[bPB[6 + h]], writes=[bABb])
    for a in range(8):
        for h in range(2):
            bank = 4 + (a * 2 + h) % 2
            p.pe(lambda e, a=a, h=h, bank=bank: e.matmul(
                PB[bank][:], DAUXB[0:100, a * 128:(a + 1) * 128],
                BB_[0:100, h * 512:(h + 1) * 512], start=True, stop=True),
                reads=[bDAUX, bABb], writes=[bPB[bank]])
            p.dve(lambda e, a=a, h=h, bank=bank: e.tensor_tensor(
                out=WS0[:, a * 1024 + h * 512:a * 1024 + (h + 1) * 512],
                in0=WS0[:, a * 1024 + h * 512:a * 1024 + (h + 1) * 512],
                in1=PB[bank][:], op=ALU.add),
                reads=[bPB[bank], bWS0[a]], writes=[bWS0[a]])

    p.dma_in(lambda e: e.dma_start(out=LAND[0:1, :], in_=dbv), writes=[bLAND], sem='lw7')
    p.pool(lambda e: e.tensor_copy(out=BVB[:], in_=LAND[0:1, :]),
           reads=[bLAND], writes=[bBV])
    for h in range(2):
        p.pe(lambda e, h=h: e.matmul(
            PB[6 + h][:], ONESB[0:1, :], BVB[0:1, h * 512:(h + 1) * 512],
            start=True, stop=True),
            reads=[bONESB, bBV], writes=[bPB[6 + h]])
        p.act(lambda e, h=h: e.activation(
            out=B128[:, h * 512:(h + 1) * 512], in_=PB[6 + h][:],
            func=AF.Copy, scale=1.0 / 128), reads=[bPB[6 + h]], writes=[bB128])

    ws.add(dwo, WS1, bWS1, extra=[bMASK])
    ws.prime()
    for _ in range(8):
        ws.cast()

    bBETB = Buf()
    p.dma_in(lambda e: e.dma_start(out=GAMB[:], in_=dgamb), writes=[bLAMC], sem='lw3')
    p.dma_in(lambda e: e.dma_start(out=BETB[:], in_=dbetb),
             writes=[bBETB, bMASK, bLANDB], sem='lw4')

    for t4 in range(4):
        for oc in range(8):
            bank = (t4 * 8 + oc) % 4
            p.pe(lambda e, t4=t4, oc=oc, bank=bank: e.matmul(
                PB[bank][:], WG2B[:, oc * 128:(oc + 1) * 128],
                T1[0:64, t4 * 512:(t4 + 1) * 512], start=True, stop=True),
                reads=[bWG2, bT1[t4]], writes=[bPB[bank]])
            p.act(lambda e, t4=t4, oc=oc, bank=bank: e.activation(
                out=GATE[:, oc * 2048 + t4 * 512:oc * 2048 + (t4 + 1) * 512],
                in_=PB[bank][:], func=AF.Sigmoid),
                reads=[bPB[bank]], writes=[bQT[oc][t4], bGATE[oc][t4]])

    for tc in range(NTILE):
        b0 = 4 + (tc % 2) * 2
        for k in range(8):
            for h in range(2):
                p.pe(lambda e, tc=tc, k=k, h=h, b0=b0: e.matmul(
                    PB[b0 + h][:],
                    XNT[:, k * 2048 + tc * 128:k * 2048 + (tc + 1) * 128],
                    WS0[:, k * 1024 + h * 512:k * 1024 + (h + 1) * 512],
                    start=(k == 0), stop=False),
                    reads=[bWS0[k], bXNT[k][tc]], writes=[bPB[b0 + h]])
        for h in range(2):
            p.pe(lambda e, tc=tc, h=h, b0=b0: e.matmul(
                PB[b0 + h][:], ONESB[:], B128[:, h * 512:(h + 1) * 512],
                start=False, stop=True),
                reads=[bONESB, bB128], writes=[bPB[b0 + h]])
        for h in range(2):
            p.act(lambda e, tc=tc, h=h, b0=b0: e.activation(
                out=G[:, tc * 1024 + h * 512:tc * 1024 + (h + 1) * 512],
                in_=PB[b0 + h][:], func=AF.Silu),
                reads=[bPB[b0 + h]],
                writes=[bKT[tc // 2][(tc % 2) * 2 + h], bG[tc]])

    if "g" in dtaps:
        tap_toks.append(p.dma_out(lambda e: e.dma_start(out=dtaps["g"], in_=G[:]),
                  reads=[bG[c] for c in range(NCH)]))
    if "gate" in dtaps:
        tap_toks.append(p.dma_out(lambda e: e.dma_start(out=dtaps["gate"], in_=GATE[:]),
                  reads=[bGATE[oc][t] for oc in range(8) for t in range(4)]))
    if "weff" in dtaps:
        tap_toks.append(p.dma_out(lambda e: e.dma_start(out=dtaps["weff"], in_=WS0[:]),
                  reads=[bWS0[k] for k in range(8)]))

    CCs = [CC, CCo]
    C2s = [CC2e, CC2o]
    bCCs = [bCC, bCC2]
    bCCs[0].w = list(bLAMQ.w) + list(bLAMQ.r)
    bCCs[1].w = list(bLAMK.w) + list(bLAMK.r)
    bSG2.w = list(bSAB[0].w) + list(bSAB[0].r) + list(bSAB[1].w) + list(bSAB[1].r)
    bC2s = [Buf(), Buf()]
    bC2s[0].w = (list(bMASK.w) + list(bMASK.r)
                 + list(bLANDB.w) + list(bLANDB.r))
    bC2s[1].w = list(bKTM[0].w) + list(bKTM[0].r) + list(bBV.w) + list(bBV.r)
    bCCP = [[Buf() for _ in range(8)] for _ in range(2)]
    for par in range(2):
        for pp in range(8):
            bCCP[par][pp].w = list(bCCs[par].w)
    out_toks = []

    def p4c_A(tc):
        par = tc % 2
        sb_st = [0, 2][par]
        sb_b = [1, 3][par]
        CCt, C2t = CCs[par], C2s[par]
        bCCt, bC2t = bCCs[par], bC2s[par]
        for pp in range(8):
            p.pe(lambda e, tc=tc, pp=pp, sb_st=sb_st: e.matmul(
                PB[sb_st][0:1, 0:128], ONESB[:, 0:1],
                VO[:, tc * 1024 + pp * 128:tc * 1024 + (pp + 1) * 128],
                start=(pp == 0), stop=(pp == 7)),
                reads=[bONESB, bVO[tc][pp // 4]], writes=[bPB[sb_st]])
        p.act(lambda e, sb_st=sb_st: e.activation(
            out=MROW[:], in_=PB[sb_st][0:1, 0:128], func=AF.Copy, scale=1.0 / E),
            reads=[bPB[sb_st]], writes=[bROW])
        p.pe(lambda e, sb_b=sb_b: e.matmul(
            PB[sb_b][:, 0:128], ONESF[:], MROW[:], start=True, stop=True),
            reads=[bONESF, bROW], writes=[bPB[sb_b]])
        p.dve(lambda e, tc=tc, CCt=CCt, sb_b=sb_b: e.tensor_tensor(
            out=A(CCt, 0, 0, [[128, 8], [1, 128]]),
            in0=A(VO, 0, tc * 1024, [[128, 8], [1, 128]]),
            in1=A(PB[sb_b], 0, 0, [[0, 8], [1, 128]]),
            op=ALU.subtract),
            reads=bVO[tc] + [bPB[sb_b]], writes=[bCCt] + bCCP[par])
        p.act(lambda e, CCt=CCt, C2t=C2t: e.activation(
            out=C2t[:], in_=CCt[:], func=AF.Square),
            reads=[bCCt] + bCCP[par], writes=[bC2t])

    def p4c_B(tc):
        par = tc % 2
        sb_st = [0, 2][par]
        sb_b = [1, 3][par]
        CCt, C2t = CCs[par], C2s[par]
        bCCt, bC2t = bCCs[par], bC2s[par]
        for pp in range(8):
            p.pe(lambda e, pp=pp, C2t=C2t, sb_st=sb_st: e.matmul(
                PB[sb_st][0:1, 128:256], ONESB[:, 0:1],
                C2t[:, pp * 128:(pp + 1) * 128],
                start=(pp == 0), stop=(pp == 7)),
                reads=[bONESB, bC2t], writes=[bPB[sb_st]])
        p.act(lambda e, sb_st=sb_st: e.activation(
            out=VROW[:], in_=PB[sb_st][0:1, 128:256],
            func=AF.Sqrt, scale=1.0 / E, bias=EPS),
            reads=[bPB[sb_st], bEPS], writes=[bROW])
        p.pe(lambda e, sb_b=sb_b: e.matmul(
            PB[sb_b][:, 128:256], ONESF[:], VROW[:], start=True, stop=True),
            reads=[bONESF, bROW], writes=[bPB[sb_b]])
        p.dve(lambda e, sb_b=sb_b: e.reciprocal(
            out=RBI[:], in_=PB[sb_b][:, 128:256]),
            reads=[bPB[sb_b]], writes=[bB128])
        for pp in range(8):
            bP = bCCP[par][pp]
            p.dve(lambda e, CCt=CCt, pp=pp: e.scalar_tensor_tensor(
                out=CCt[:, pp * 128:(pp + 1) * 128],
                in0=CCt[:, pp * 128:(pp + 1) * 128],
                scalar=GAMB[:, pp * 128:pp * 128 + 1],
                in1=RBI[:], op0=ALU.mult, op1=ALU.mult),
                reads=[bP, bB128, bLAMC], writes=[bP])
            p.dve(lambda e, tc=tc, CCt=CCt, pp=pp: e.scalar_tensor_tensor(
                out=CCt[:, pp * 128:(pp + 1) * 128],
                in0=CCt[:, pp * 128:(pp + 1) * 128],
                scalar=BETB[:, pp * 128:pp * 128 + 1],
                in1=GATE[:, pp * 2048 + tc * 128:pp * 2048 + (tc + 1) * 128],
                op0=ALU.add, op1=ALU.mult),
                reads=[bP, bBETB, bGATE[pp][tc // 4]], writes=[bP])

    def p4c_S(tc):
        par = tc % 2
        sb0 = 4 + par * 2
        CCt, bCCt = CCs[par], bCCs[par]
        for ec in range(8):
            for h in range(2):
                p.pe(lambda e, ec=ec, h=h, sb0=sb0, CCt=CCt: e.matmul(
                    PB[sb0 + h][:], CCt[:, ec * 128:(ec + 1) * 128],
                    WS1[:, ec * 1024 + h * 512:ec * 1024 + (h + 1) * 512],
                    start=(ec == 0), stop=(ec == 7)),
                    reads=[bCCP[par][ec], bWS1[ec]], writes=[bPB[sb0 + h]])
        sgt, sgb = (SGT, bSG) if tc % 2 == 0 else (SGT2, bSG2)
        for h in range(2):
            p.dve(lambda e, tc=tc, h=h, sb0=sb0, sgt=sgt: e.tensor_tensor(
                out=sgt[:, h * 512:(h + 1) * 512], in0=PB[sb0 + h][:],
                in1=G[:, tc * 1024 + h * 512:tc * 1024 + (h + 1) * 512],
                op=ALU.mult),
                reads=[bPB[sb0 + h], bG[tc]], writes=[sgb])
        tok = p.dma_out(lambda e, tc=tc, sgt=sgt: e.dma_start(
            out=dy[tc * 128:(tc + 1) * 128, :], in_=sgt[:],
            accum_op=ALU.add), reads=[sgb, bDY])
        out_toks.append(tok)

    p4c_A(0)
    p4c_B(0)
    for tc in range(1, NTILE):
        p4c_A(tc)
        p4c_S(tc - 1)
        p4c_B(tc)
    p4c_S(NTILE - 1)

    p.final_wait(out_toks)
    p.replay()
    return nc



def _bcast_affine(v):
    g = np.asarray(v, dtype=np.float32).reshape(8, 128).T.astype(BF)
    return np.ascontiguousarray(
        np.broadcast_to(g[:, :, None], (128, 8, 128)).reshape(128, 1024))


_NC_CACHE = {}


def kernel(L0_attn_x, L0_attn_Wq, L0_attn_Wk, L0_attn_Wv, L0_attn_Wo,
           L0_attn_gamma, L0_attn_beta, L0_attn_Wg1, L0_attn_Wg2,
           L0_ffn_W, L0_ffn_b, L0_ffn_lora_down, L0_ffn_lora_up,
           L0_ffn_down_aux, L0_ffn_up_aux):
    taps = tuple(DEBUG_TAPS)
    key = (taps, STOP_STAGE)
    if key not in _NC_CACHE:
        _NC_CACHE[key] = build(taps, STOP_STAGE)
    nc = _NC_CACHE[key]

    cst = _consts()
    f32 = np.float32
    shared = {
        "wq": np.ascontiguousarray(L0_attn_Wq, dtype=f32),
        "wk": np.ascontiguousarray(L0_attn_Wk, dtype=f32),
        "wv": np.ascontiguousarray(L0_attn_Wv, dtype=f32),
        "wo": np.ascontiguousarray(L0_attn_Wo, dtype=f32),
        "w": np.ascontiguousarray(L0_ffn_W, dtype=f32),
        "wg1": np.ascontiguousarray(L0_attn_Wg1, dtype=f32),
        "wg2": np.ascontiguousarray(L0_attn_Wg2, dtype=f32),
        "gamb": _bcast_affine(L0_attn_gamma),
        "betb": _bcast_affine(L0_attn_beta),
        "bvec": np.ascontiguousarray(np.asarray(L0_ffn_b, dtype=f32).reshape(1, E)),
        "ldown": np.ascontiguousarray(L0_ffn_lora_down, dtype=f32),
        "lup": np.ascontiguousarray(L0_ffn_lora_up, dtype=f32),
        "daux": np.ascontiguousarray(L0_ffn_down_aux, dtype=f32),
        "uaux": np.ascontiguousarray(L0_ffn_up_aux, dtype=f32),
        "maskt": cst["maskt"], "lamqb": cst["lamqb"], "lamkb": cst["lamkb"],
        "lamcb": cst["lamcb"], "identb": cst["identb"],
        "onesb": cst["onesb"], "onesf": cst["onesf"],
    }
    xs = np.ascontiguousarray(np.asarray(L0_attn_x, dtype=f32))
    in_maps = [dict(shared, x=xs[b]) for b in range(8)]
    res = run_bass_kernel_spmd(nc, in_maps, core_ids=list(range(8)))
    out = np.stack([np.asarray(res.results[b]["y"], dtype=f32) for b in range(8)])
    if taps:
        kernel.last_results = res
    return out


# version 32
# speedup vs baseline: 1.0111x; 1.0111x over previous
import numpy as np
import ml_dtypes
import concourse.bass as bass
import concourse.mybir as mybir
from concourse.bass_utils import run_bass_kernel_spmd

F32 = mybir.dt.float32
BF16 = mybir.dt.bfloat16
AF = mybir.ActivationFunctionType
ALU = mybir.AluOpType

T = 2048
E = 1024
H = 16
DH = 64
CH = 128
NCH = T // CH
NPAIR = H // 2
NTILE = T // 128
EPS = 1e-5

BF = np.dtype(ml_dtypes.bfloat16)

DEBUG_TAPS = []
STOP_STAGE = 4



STREAMS = ("sync", "gpsimd", "vector", "scalar", "tensor")


class Buf:
    __slots__ = ("w", "r")

    def __init__(self):
        self.w = []
        self.r = []


class Prog:
    def __init__(self, nc):
        self.nc = nc
        self.ops = {s: [] for s in STREAMS}
        names = (["pe", "act", "dve", "pool", "sqc", "pcp"]
                 + [f"sq{i}" for i in range(4)] + [f"gq{i}" for i in range(4)]
                 + [f"xq{i}" for i in range(10)] + [f"lw{i}" for i in range(8)])
        self.sems = {n: [nc.alloc_semaphore("s_" + n), 0] for n in names}
        self.waited = {s: {n: 0 for n in names} for s in STREAMS}
        self.sq_rr = 0
        self.gq_rr = 0

    def _emit(self, stream, sem, step, fn, reads=(), writes=(), extra=()):
        deps = list(extra)
        for b in reads:
            deps += b.w
        for b in writes:
            deps += b.w + b.r
        seen = {}
        for (sn, sv) in deps:
            if sv > seen.get(sn, 0):
                seen[sn] = sv
        waits = []
        for sn, sv in seen.items():
            if stream == "tensor" and sn == "pe":
                continue
            if self.waited[stream][sn] < sv:
                waits.append((sn, sv))
                self.waited[stream][sn] = sv
        rec = self.sems[sem]
        if step:
            rec[1] += step
        tok = (sem, rec[1])
        self.ops[stream].append((fn, waits, sem, step))
        for b in reads:
            b.r.append(tok)
        for b in writes:
            b.w = [tok]
            b.r = []
        return tok

    def pe(self, fn, reads=(), writes=(), extra=()):
        return self._emit("tensor", "pe", 1, fn, reads, writes, extra)

    def act(self, fn, reads=(), writes=(), extra=()):
        return self._emit("scalar", "act", 1, fn, reads, writes, extra)

    def dve(self, fn, reads=(), writes=(), extra=()):
        return self._emit("vector", "dve", 1, fn, reads, writes, extra)

    def pool(self, fn, reads=(), writes=(), extra=()):
        return self._emit("gpsimd", "pool", 1, fn, reads, writes, extra)

    def dma_in(self, fn, reads=(), writes=(), extra=(), const=False, sem=None):
        if sem is None:
            if const:
                sem = "sqc"
            else:
                sem = f"sq{self.sq_rr % 4}"
                self.sq_rr += 1
        return self._emit("sync", sem, 16, fn, reads, writes, extra)

    def dma_vec(self, fn, sub, reads=(), writes=(), extra=()):
        return self._emit("scalar", f"xq{sub}", 16, fn, reads, writes, extra)

    def dma_pcp(self, fn, reads=(), writes=(), extra=()):
        return self._emit("gpsimd", "pcp", 16, fn, reads, writes, extra)

    def dma_out(self, fn, reads=(), writes=(), extra=()):
        sem = f"gq{self.gq_rr % 4}"
        self.gq_rr += 1
        return self._emit("gpsimd", sem, 16, fn, reads, writes, extra)

    def final_wait(self, tokens):
        self._emit("gpsimd", "pool", 0, None, extra=tokens)

    def replay(self):
        with self.nc.Block() as block:
            def runner(stream):
                def _run(eng):
                    for fn, waits, sem, step in self.ops[stream]:
                        for (sn, sv) in waits:
                            eng.wait_ge(self.sems[sn][0], sv)
                        if fn is not None:
                            ins = fn(eng)
                            if step:
                                ins.then_inc(self.sems[sem][0], step)
                return _run
            block.sync(runner("sync"))
            block.gpsimd(runner("gpsimd"))
            block.vector(runner("vector"))
            block.scalar(runner("scalar"))
            block.tensor(runner("tensor"))


def A(ten, row0, col0, dims, rowcount=None):
    stride = ten.shape[1]
    if rowcount is None:
        rowcount = ten.shape[0] - row0
    return bass.AP(ten, row0 * stride + col0, [[stride, rowcount]] + list(dims))



def _build_slopes(n):
    def po2(m):
        start = 2.0 ** (-(2.0 ** (-(np.log2(m) - 3))))
        return [start * start ** i for i in range(m)]
    if np.log2(n).is_integer():
        return po2(n)
    closest = 2 ** int(np.floor(np.log2(n)))
    return po2(closest) + _build_slopes(2 * closest)[0::2][: n - closest]


def _consts():
    s = np.array(_build_slopes(H), dtype=np.float32)
    s = (s * np.float32(1.0 + 1e-5)).astype(np.float64)
    j = np.arange(CH, dtype=np.float64)
    diff = j[None, :, None] * 0 + (j[None, None, :] - j[None, :, None])
    m = np.where(diff >= 0, np.exp(-s[:, None, None] * diff), 0.0)
    maskt = np.transpose(m, (1, 0, 2)).reshape(CH, H * CH).astype(BF)
    lamq = np.exp(-s[:, None] * (j + 1.0)[None, :])
    lamqb = np.zeros((128, NPAIR * CH))
    for p in range(NPAIR):
        lamqb[0:64, p * CH:(p + 1) * CH] = lamq[2 * p][None, :]
        lamqb[64:128, p * CH:(p + 1) * CH] = lamq[2 * p + 1][None, :]
    lamk = np.exp(-s[:, None] * (CH - 1.0 - j)[None, :])
    lamkb = np.zeros((128, E))
    for h in range(H):
        lamkb[:, h * DH:(h + 1) * DH] = lamk[h][:, None]
    lamc = np.exp(-s * CH)
    lamcb = np.zeros((128, NPAIR * CH))
    for p in range(NPAIR):
        lamcb[0:64, p * CH:(p + 1) * CH] = lamc[2 * p]
        lamcb[64:128, p * CH:(p + 1) * CH] = lamc[2 * p + 1]
    return dict(
        maskt=maskt,
        lamqb=lamqb.astype(BF), lamkb=lamkb.astype(BF), lamcb=lamcb.astype(BF),
        identb=np.eye(128, dtype=BF),
        onesb=np.ones((128, 128), dtype=BF),
        onesf=np.ones((1, 128), dtype=np.float32))



def build(debug_taps=(), stage=4):
    nc = bass.Bass("TRN2", target_bir_lowering=False)

    dx = nc.dram_tensor("x", [T, E], F32, kind="ExternalInput").ap()
    dwq = nc.dram_tensor("wq", [E, E], F32, kind="ExternalInput").ap()
    dwk = nc.dram_tensor("wk", [E, E], F32, kind="ExternalInput").ap()
    dwv = nc.dram_tensor("wv", [E, E], F32, kind="ExternalInput").ap()
    dwo = nc.dram_tensor("wo", [E, E], F32, kind="ExternalInput").ap()
    dw = nc.dram_tensor("w", [E, E], F32, kind="ExternalInput").ap()
    dwg1h = nc.dram_tensor("wg1", [E, DH], F32, kind="ExternalInput")
    dwg2 = nc.dram_tensor("wg2", [DH, E], F32, kind="ExternalInput").ap()
    dgamb = nc.dram_tensor("gamb", [128, 1024], BF16, kind="ExternalInput").ap()
    dbetb = nc.dram_tensor("betb", [128, 1024], BF16, kind="ExternalInput").ap()
    dbv = nc.dram_tensor("bvec", [1, E], F32, kind="ExternalInput").ap()
    dldown = nc.dram_tensor("ldown", [4, 100], F32, kind="ExternalInput").ap()
    dlup = nc.dram_tensor("lup", [50, 4], F32, kind="ExternalInput").ap()
    ddaux = nc.dram_tensor("daux", [100, E], F32, kind="ExternalInput").ap()
    duauxh = nc.dram_tensor("uaux", [E, 50], F32, kind="ExternalInput")
    dmaskt = nc.dram_tensor("maskt", [128, H * CH], BF16, kind="ExternalInput").ap()
    dlamqb = nc.dram_tensor("lamqb", [128, NPAIR * CH], BF16, kind="ExternalInput").ap()
    dlamkb = nc.dram_tensor("lamkb", [128, E], BF16, kind="ExternalInput").ap()
    dlamcb = nc.dram_tensor("lamcb", [128, NPAIR * CH], BF16, kind="ExternalInput").ap()
    didb = nc.dram_tensor("identb", [128, 128], BF16, kind="ExternalInput").ap()
    donesb = nc.dram_tensor("onesb", [128, 128], BF16, kind="ExternalInput").ap()
    donesf = nc.dram_tensor("onesf", [1, 128], F32, kind="ExternalInput").ap()
    dy = nc.dram_tensor("y", [T, E], F32, kind="ExternalOutput").ap()
    dtaps = {}
    for tname in debug_taps:
        shp, dt = {"xnt": ([128, 16384], BF16), "qt": ([128, 16384], BF16),
                   "kt": ([128, 16384], BF16), "vo": ([128, 16384], BF16),
                   "ot": ([128, 16384], BF16),
                   "t1": ([128, 2048], BF16), "g": ([128, 16384], BF16),
                   "gate": ([128, 16384], BF16), "weff": ([128, 8192], BF16),
                   "kv1": ([128, 1024], BF16), "qh1": ([128, 1024], BF16),
                   "ws0": ([128, 8192], BF16),
                   "ktm0": ([128, 1024], BF16),
                   }[tname]
        dtaps[tname] = nc.dram_tensor("tap_" + tname, shp, dt,
                                      kind="ExternalOutput").ap()

    off = {}
    cur = 16768

    def place(name, nbytes, align=64):
        nonlocal cur
        cur = (cur + align - 1) // align * align
        off[name] = cur
        cur += nbytes

    place("XNT", 32768)
    place("QT", 32768)
    place("KT", 32768)
    place("VO", 32768)
    place("WS0", 16384)
    place("WS1", 16384)
    place("LAND", 4096)
    place("XNTM", 4096)
    place("KTM", 4096)
    place("QH", 4096)
    place("SAB", 4096)
    place("KVBD", 4096)
    place("MASKT", 4096)
    place("LAMQB", 2048)
    place("LAMKB", 2048)
    place("LAMCB", 2048)
    place("T1", 4096)
    place("WG1", 1024)
    place("WG2", 2048)
    place("IDB", 256)
    place("ONESB", 256)
    place("ONESF", 512)
    place("GBP", 64)
    place("SMALL", 3072)
    place("LSM", 1024)
    assert cur <= 229344, f"sbuf over budget: {cur}"

    mk = nc.alloc_sbuf_tensor_at

    XNT = mk("XNT", [128, 16384], BF16, offset=off["XNT"])
    QT = mk("QT", [128, 16384], BF16, offset=off["QT"])
    GATE = mk("GATE", [128, 16384], BF16, offset=off["QT"])
    KT = mk("KT", [128, 16384], BF16, offset=off["KT"])
    G = mk("G", [128, 16384], BF16, offset=off["KT"])
    VO = mk("VO", [128, 16384], BF16, offset=off["VO"])
    WS0 = mk("WS0", [128, 8192], BF16, offset=off["WS0"])
    WS1 = mk("WS1", [128, 8192], BF16, offset=off["WS1"])
    LAND = mk("LAND", [128, 1024], F32, offset=off["LAND"])
    XNTM = mk("XNTM", [128, 2048], BF16, offset=off["XNTM"])
    XV2 = mk("XV2", [128, 2048], F32, offset=off["VO"])
    XV3 = mk("XV3", [128, 6144], F32, offset=off["VO"] + 8192)
    WGL1 = mk("WGL1", [128, 512], F32, offset=off["XNTM"])
    WGL2 = mk("WGL2", [64, 1024], F32, offset=off["XNTM"])
    KTM = mk("KTM", [128, 2048], BF16, offset=off["KTM"])
    BVB = mk("BVB", [1, 1024], BF16, offset=off["KTM"])
    B128 = mk("B128", [128, 1024], BF16, offset=off["KTM"] + 2048)
    RBI = mk("RBI", [128, 128], F32, offset=off["KTM"] + 2048)
    QH = mk("QH", [128, 2048], BF16, offset=off["QH"])
    SGT = mk("SGT", [128, 1024], F32, offset=off["QH"])
    SGT2 = mk("SGT2", [128, 1024], F32, offset=off["SAB"])
    XTS = mk("XTS", [128, 2048], F32, offset=off["SAB"])
    SAB = mk("SAB", [128, 2048], BF16, offset=off["SAB"])
    AB_ = mk("AB_", [4, 1024], BF16, offset=off["SAB"])
    BB_ = mk("BB_", [100, 1024], BF16, offset=off["SAB"] + 2048)
    KVBD = mk("KVBD", [128, 2048], BF16, offset=off["KVBD"])
    DAUXB = mk("DAUXB", [100, 1024], BF16, offset=off["KVBD"])
    UAT = mk("UAT", [50, 1024], BF16, offset=off["KVBD"] + 2048)
    MASKT = mk("MASKT", [128, 2048], BF16, offset=off["MASKT"])
    LANDB = mk("LANDB", [128, 1024], F32, offset=off["MASKT"])
    BETB = mk("BETB", [128, 1024], BF16, offset=off["MASKT"])
    CC2e = mk("CC2e", [128, 1024], BF16, offset=off["MASKT"] + 2048)
    LAMQB = mk("LAMQB", [128, 1024], BF16, offset=off["LAMQB"])
    CC = mk("CC", [128, 1024], BF16, offset=off["LAMQB"])
    LAMKB = mk("LAMKB", [128, 1024], BF16, offset=off["LAMKB"])
    CCo = mk("CCo", [128, 1024], BF16, offset=off["LAMKB"])
    CC2o = mk("CC2o", [128, 1024], BF16, offset=off["KTM"])
    LAMCB = mk("LAMCB", [128, 1024], BF16, offset=off["LAMCB"])
    GAMB = mk("GAMB", [128, 1024], BF16, offset=off["LAMCB"])
    T1 = mk("T1", [128, 2048], BF16, offset=off["T1"])
    WG1B = mk("WG1B", [128, 512], BF16, offset=off["WG1"])
    WG2B = mk("WG2B", [64, 1024], BF16, offset=off["WG2"])
    IDB = mk("IDB", [128, 128], BF16, offset=off["IDB"])
    ONESB = mk("ONESB", [128, 128], BF16, offset=off["ONESB"])
    ONESF = mk("ONESF", [1, 128], F32, offset=off["ONESF"])
    SUMS = mk("SUMS", [128, 16], F32, offset=off["SMALL"])
    SUMSI = mk("SUMSI", [128, 16], mybir.dt.int32, offset=off["SMALL"])
    MROW = mk("MROW", [1, 128], F32, offset=off["SMALL"] + 64)
    VROW = mk("VROW", [1, 128], F32, offset=off["SMALL"] + 576)
    RSROW = mk("RSROW", [1, 128], F32, offset=off["SMALL"] + 1088)
    LUPF = mk("LUPF", [50, 4], F32, offset=off["SMALL"] + 1600)
    LDOWNF = mk("LDOWNF", [4, 100], F32, offset=off["SMALL"] + 1632)
    LSM1 = mk("LSM1", [50, 4], BF16, offset=off["LSM"])
    LSM2 = mk("LSM2", [4, 100], BF16, offset=off["LSM"] + 32)
    LSM3 = mk("LSM3", [128, 400], BF16, offset=off["SMALL"] + 2048)
    EPSC = mk("EPSC", [128, 1], F32, offset=off["SMALL"] + 2848)
    BNS = mk("BNS", [128, 24], F32, offset=off["SMALL"] + 2880)
    nc.const_aps.register(F32, EPS, EPSC[:])

    PB = [nc.alloc_psum_tensor(f"pb{i}", [128, 512], F32) for i in range(8)]

    def pbh(i):
        return PB[i][:].bitcast(BF16)

    p = Prog(nc)

    bXNTM = [[Buf(), Buf()], [Buf(), Buf()]]
    bSUM = [Buf(), Buf()]
    bXNT = [[Buf() for _ in range(NTILE)] for _ in range(8)]
    bPB = [Buf() for _ in range(8)]
    bLAND = Buf()
    bWS0 = [Buf() for _ in range(8)]
    bWS1 = [Buf() for _ in range(8)]
    bQT = [[Buf() for _ in range(4)] for _ in range(8)]
    bKT = [[Buf() for _ in range(4)] for _ in range(8)]
    bVO = [[Buf(), Buf()] for _ in range(NCH)]
    bT1 = [Buf() for _ in range(4)]
    bKTM = [[Buf(), Buf()], [Buf(), Buf()]]
    bQH = [Buf(), Buf()]
    bSAB = [Buf(), Buf()]
    bKV = [Buf(), Buf()]
    bG = [Buf() for _ in range(NCH)]
    bGATE = [[Buf() for _ in range(4)] for _ in range(8)]
    bWG1 = Buf(); bWG2 = Buf()
    bMASK = Buf(); bLAMQ = Buf(); bLAMK = Buf(); bLAMC = Buf()
    bIDB = Buf(); bONESB = Buf(); bONESF = Buf()
    bLSM = Buf()
    bBV = Buf(); bB128 = Buf()
    bDAUX = bKV[0]; bUAT = bKV[1]
    bABa = bSAB[0]; bABb = bSAB[1]
    bXV = [Buf() for _ in range(8)]
    bROW = Buf()
    bCC = Buf(); bCC2 = Buf(); bSG2 = Buf()
    bSG = Buf()

    for dstap, src, b in ((LAMQB[:], dlamqb, bLAMQ),
                          (LAMKB[:], dlamkb, bLAMK), (LAMCB[:], dlamcb, bLAMC),
                          (IDB[:], didb, bIDB),
                          (ONESB[:], donesb, bONESB), (ONESF[:], donesf, bONESF),
                          (LUPF[:], dlup, bLSM), (LDOWNF[:], dldown, bLSM)):
        p.dma_in(lambda e, d=dstap, s=src: e.dma_start(out=d, in_=s),
                 writes=[b], const=True)
    _ctot = p.sems["sqc"][1]
    for _b in (bLAMQ, bLAMK, bLAMC, bIDB, bONESB, bONESF, bLSM):
        _b.w = [("sqc", _ctot)]
    bEPS = Buf()
    p.dve(lambda e: e.memset(EPSC[:], EPS), writes=[bEPS])

    bLANDB = Buf()

    class WStream:
        def __init__(self):
            self.chunks = []
            self.nd = 0
            self.nc = 0
            self.extra = {}

        def add(self, dsrc, dten, bufs, extra=()):
            for k in range(8):
                if extra:
                    self.extra[len(self.chunks)] = list(extra)
                self.chunks.append((dsrc, k, dten, bufs))

        def _dma(self):
            i = self.nd
            if i >= len(self.chunks):
                return
            dsrc, k, _, _ = self.chunks[i]
            land, lb = (LAND, bLAND) if i % 2 == 0 else (LANDB, bLANDB)
            p.dma_in(lambda e, k=k, land=land, dsrc=dsrc: e.dma_start(
                out=land[:], in_=dsrc[k * 128:(k + 1) * 128, :]),
                writes=[lb] + self.extra.get(i, []))
            self.nd += 1

        def prime(self):
            while self.nd < min(self.nc + 2, len(self.chunks)):
                self._dma()

        def cast(self):
            i = self.nc
            _, k, dten, bufs = self.chunks[i]
            land, lb = (LAND, bLAND) if i % 2 == 0 else (LANDB, bLANDB)
            p.dve(lambda e, k=k, dten=dten, land=land: e.tensor_copy(
                out=dten[:, k * 1024:(k + 1) * 1024], in_=land[:]),
                reads=[lb], writes=[bufs[k]])
            self.nc += 1
            self._dma()

    ws = WStream()
    ws.add(dwq, WS0, bWS0)
    ws.add(dwk, WS1, bWS1)
    ws.add(dwv, WS0, bWS0)
    ws.add(dw, WS1, bWS1)
    ws.prime()

    def xslot(t):
        s = t if t < 10 else t - 10
        if s == 0:
            return XTS[:, 0:1024], [bSAB[0], bSAB[1]]
        if s == 1:
            return XTS[:, 1024:2048], [bKV[0], bKV[1]]
        if s < 4:
            return XV2[:, (s - 2) * 1024:(s - 1) * 1024], [bXV[s - 2]]
        return XV3[:, (s - 4) * 1024:(s - 3) * 1024], [bXV[s - 2]]

    def emit_xdma(t):
        xt, bb = xslot(t)
        p.dma_vec(lambda e, t=t, xt=xt: e.dma_start(
            out=xt, in_=dx[t * 128:(t + 1) * 128, :]),
            t if t < 10 else t - 10, writes=bb)

    def proj_fmajor_t4(wten, wbufs, dstten, dstbufs, act_fn, t4):
        for oc in range(8):
            bank = 4 + (t4 * 8 + oc) % 4
            for k in range(8):
                need = [bXNT[k][t4 * 4 + i] for i in range(4)]
                p.pe(lambda e, k=k, oc=oc, t4=t4, bank=bank, wten=wten: e.matmul(
                    PB[bank][:],
                    wten[:, k * 1024 + oc * 128:k * 1024 + (oc + 1) * 128],
                    XNT[:, k * 2048 + t4 * 512:k * 2048 + (t4 + 1) * 512],
                    start=(k == 0), stop=(k == 7)),
                    reads=[wbufs[k]] + need, writes=[bPB[bank]])
            p.act(lambda e, oc=oc, t4=t4, bank=bank, dstten=dstten: e.activation(
                out=dstten[:, oc * 2048 + t4 * 512:oc * 2048 + (t4 + 1) * 512],
                in_=PB[bank][:], func=act_fn),
                reads=[bPB[bank]], writes=[dstbufs[oc][t4]])

    def p1_stats(t):
        sl = t % 2
        xt, xb = xslot(t)
        m = SUMS[:, 2 * sl:2 * sl + 1]
        va = SUMS[:, 2 * sl + 1:2 * sl + 2]
        sd = SUMS[:, 8 + sl:9 + sl]
        yv = SUMS[:, 10 + sl:11 + sl]
        for h in range(2):
            p.dve(lambda e, xt=xt, sl=sl, h=h: e.bn_stats(
                out=BNS[:, sl * 12 + h * 6:sl * 12 + h * 6 + 6],
                in_=xt[:, h * 512:(h + 1) * 512]),
                reads=xb, writes=[bSUM[sl]])
        p.dve(lambda e, sl=sl: e.bn_aggr(
            out=SUMS[:, 2 * sl:2 * sl + 2],
            in_=BNS[:, sl * 12:sl * 12 + 12]),
            reads=[bSUM[sl]], writes=[bSUM[sl]])
        p.act(lambda e, va=va, sd=sd: e.activation(
            out=sd, in_=va, func=AF.Sqrt, bias=EPS),
            reads=[bSUM[sl], bEPS], writes=[bSUM[sl]])
        p.dve(lambda e, sd=sd, yv=yv: e.reciprocal(out=yv, in_=sd),
              reads=[bSUM[sl]], writes=[bSUM[sl]])
        xn = XNTM[:, sl * 1024:(sl + 1) * 1024]
        for h in range(2):
            p.dve(lambda e, xt=xt, m=m, yv=yv, xn=xn, h=h: e.tensor_scalar(
                out=xn[:, h * 512:(h + 1) * 512], in0=xt[:, h * 512:(h + 1) * 512],
                scalar1=m, scalar2=yv, op0=ALU.subtract, op1=ALU.mult),
                reads=xb + [bSUM[sl]], writes=[bXNTM[sl][h]])
        if t + 10 < NTILE:
            emit_xdma(t + 10)

    def p1_transpose(t):
        sl = t % 2
        xn = XNTM[:, sl * 1024:(sl + 1) * 1024]
        bank = t % 4
        for f in range(8):
            p.pe(lambda e, f=f, xn=xn, bank=bank: e.matmul(
                pbh(bank)[:, f * 128:(f + 1) * 128],
                xn[:, f * 128:(f + 1) * 128], IDB[:],
                start=(f == 0), stop=(f == 7), is_transpose=True),
                reads=[bXNTM[sl][f // 4], bIDB], writes=[bPB[bank]])
        dst = A(XNT, 0, t * 128, [[2048, 8], [1, 128]])
        p.act(lambda e, bank=bank, dst=dst: e.activation(
            out=dst, in_=pbh(bank)[:], func=AF.Copy),
            reads=[bPB[bank]], writes=[bXNT[f][t] for f in range(8)])

    bDY = Buf()
    for _t in range(10):
        emit_xdma(_t)
    p1_stats(0)
    ws.cast(); ws.cast()
    for t in range(1, NTILE):
        p1_stats(t)
        if t < 12:
            ws.cast()
        if t < 4:
            ws.cast()
        p1_transpose(t - 1)
        if (t - 1) % 4 == 3 and t - 1 < 12:
            proj_fmajor_t4(WS0, bWS0, QT, bQT, AF.Silu, (t - 1) // 4)
        if t - 1 >= 12:
            proj_fmajor_t4(WS1, bWS1, KT, bKT, AF.Silu, t - 13)
    p1_transpose(NTILE - 1)
    proj_fmajor_t4(WS0, bWS0, QT, bQT, AF.Silu, 3)
    proj_fmajor_t4(WS1, bWS1, KT, bKT, AF.Silu, 3)

    for _ in range(8):
        ws.cast()

    srcg1 = bass.AP(dwg1h, 0, [[64, 128], [8192, 8], [1, 64]])
    p.dma_in(lambda e, s=srcg1: e.dma_start(out=WGL1[:], in_=s),
             writes=bXNTM[0] + bXNTM[1], sem='lw0')
    p.pool(lambda e: e.tensor_copy(out=WG1B[:], in_=WGL1[:]),
           reads=[bXNTM[0][0]], writes=[bWG1])
    p.dma_in(lambda e: e.dma_start(out=WGL2[:], in_=dwg2),
             writes=bXNTM[0] + bXNTM[1], sem='lw1')
    p.pool(lambda e: e.tensor_copy(out=WG2B[:], in_=WGL2[:]),
           reads=[bXNTM[0][0]], writes=[bWG2])

    for _ in range(8):
        ws.cast()
    for tc in range(NTILE):
        for h2 in range(2):
            bank = 4 + (tc * 2 + h2) % 4
            for k in range(8):
                p.pe(lambda e, tc=tc, h2=h2, k=k, bank=bank: e.matmul(
                    PB[bank][:],
                    XNT[:, k * 2048 + tc * 128:k * 2048 + (tc + 1) * 128],
                    WS0[:, k * 1024 + h2 * 512:k * 1024 + (h2 + 1) * 512],
                    start=(k == 0), stop=(k == 7)),
                    reads=[bWS0[k], bXNT[k][tc]], writes=[bPB[bank]])
            xw = [bXV[tc // 2]]
            p.act(lambda e, tc=tc, h2=h2, bank=bank: e.activation(
                out=VO[:, tc * 1024 + h2 * 512:tc * 1024 + (h2 + 1) * 512],
                in_=PB[bank][:], func=AF.Silu),
                reads=[bPB[bank]], writes=[bVO[tc][h2]] + xw)

    for t4 in range(4):
        bank = 4 + t4 % 4
        for k in range(8):
            need = [bXNT[k][t4 * 4 + i] for i in range(4)]
            p.pe(lambda e, k=k, t4=t4, bank=bank: e.matmul(
                PB[bank][0:64, :],
                WG1B[:, k * 64:(k + 1) * 64],
                XNT[:, k * 2048 + t4 * 512:k * 2048 + (t4 + 1) * 512],
                start=(k == 0), stop=(k == 7)),
                reads=[bWG1] + need, writes=[bPB[bank]])
        p.act(lambda e, t4=t4, bank=bank: e.activation(
            out=T1[0:64, t4 * 512:(t4 + 1) * 512], in_=PB[bank][0:64, :],
            func=AF.Copy), reads=[bPB[bank]], writes=[bT1[t4]])

    tap_toks = []
    if "xnt" in dtaps:
        tap_toks.append(p.dma_out(lambda e: e.dma_start(out=dtaps["xnt"], in_=XNT[:]),
                  reads=[bXNT[f][t] for f in range(8) for t in range(NTILE)]))
    if "qt" in dtaps:
        tap_toks.append(p.dma_out(lambda e: e.dma_start(out=dtaps["qt"], in_=QT[:]),
                  reads=[bQT[oc][t] for oc in range(8) for t in range(4)]))
    if "kt" in dtaps:
        tap_toks.append(p.dma_out(lambda e: e.dma_start(out=dtaps["kt"], in_=KT[:]),
                  reads=[bKT[oc][t] for oc in range(8) for t in range(4)]))
    if "vo" in dtaps:
        tap_toks.append(p.dma_out(lambda e: e.dma_start(out=dtaps["vo"], in_=VO[:]),
                  reads=[b for c in range(NCH) for b in bVO[c]]))
    if "t1" in dtaps:
        tap_toks.append(p.dma_out(lambda e: e.dma_start(out=dtaps["t1"], in_=T1[:]),
                  reads=[bT1[t] for t in range(4)]))

    if stage < 3:
        p.final_wait(tap_toks)
        p.replay()
        return nc

    p.dma_in(lambda e: e.dma_start(out=MASKT[:], in_=dmaskt),
             writes=[bMASK, bLANDB], sem='lw2')
    p.dma_pcp(lambda e: e.dma_start(out=dy, in_=dx), writes=[bDY])

    p.dve(lambda e: e.memset(KVBD[:, 0:1024], 0.0),
          reads=(), writes=[bKV[0]])
    for c in range(NCH):
        csl = c % 2
        nsl = (c + 1) % 2
        for w in range(2):
            sa, sb = 1 + 2 * w, 2 + 2 * w
            for i in range(4):
                pp = w * 4 + i
                p.pe(lambda e, pp=pp, i=i, c=c, sa=sa: e.matmul(
                    PB[sa][:, i * 128:(i + 1) * 128],
                    A(KT, 0, pp * 2048 + c * 128, [[1, 128]], rowcount=64),
                    A(QT, 0, pp * 2048 + c * 128, [[1, 128]], rowcount=64),
                    start=(i == 0), stop=(i == 3), tile_position=(0, 0)),
                    reads=[bKT[pp][c // 4], bQT[pp][c // 4]], writes=[bPB[sa]])
                p.pe(lambda e, pp=pp, i=i, c=c, sb=sb: e.matmul(
                    PB[sb][:, i * 128:(i + 1) * 128],
                    A(KT, 64, pp * 2048 + c * 128, [[1, 128]]),
                    A(QT, 64, pp * 2048 + c * 128, [[1, 128]]),
                    start=(i == 0), stop=(i == 3), tile_position=(64, 0)),
                    reads=[bKT[pp][c // 4], bQT[pp][c // 4]], writes=[bPB[sb]])
        for w in range(2):
            sa, sb = 1 + 2 * w, 2 + 2 * w
            p.dve(lambda e, sa=sa, w=w: e.tensor_tensor(
                out=A(SAB, 0, w * 1024, [[256, 4], [1, 128]]),
                in0=A(PB[sa], 0, 0, [[128, 4], [1, 128]]),
                in1=A(MASKT, 0, w * 1024, [[256, 4], [1, 128]]),
                op=ALU.mult),
                reads=[bPB[sa], bMASK], writes=[bSAB[w]])
            p.dve(lambda e, sb=sb, w=w: e.tensor_tensor(
                out=A(SAB, 0, w * 1024 + 128, [[256, 4], [1, 128]]),
                in0=A(PB[sb], 0, 0, [[128, 4], [1, 128]]),
                in1=A(MASKT, 0, w * 1024 + 128, [[256, 4], [1, 128]]),
                op=ALU.mult),
                reads=[bPB[sb], bMASK], writes=[bSAB[w]])
        for f in range(8):
            p.pe(lambda e, c=c, f=f: e.matmul(
                pbh(0)[:, f * 128:(f + 1) * 128],
                KT[:, f * 2048 + c * 128:f * 2048 + (c + 1) * 128], IDB[:],
                start=(f == 0), stop=(f == 7), is_transpose=True),
                reads=[bKT[f][c // 4], bIDB], writes=[bPB[0]])
        for hf in range(2):
            p.dve(lambda e, csl=csl, hf=hf: e.tensor_tensor(
                out=KTM[:, csl * 1024 + hf * 512:csl * 1024 + (hf + 1) * 512],
                in0=pbh(0)[:, hf * 512:(hf + 1) * 512],
                in1=LAMKB[:, hf * 512:(hf + 1) * 512], op=ALU.mult),
                reads=[bPB[0], bLAMK], writes=[bKTM[csl][hf]])
        qsrc = A(QT, 0, c * 128, [[2048, 8], [1, 128]])
        qdst = A(QH, 0, csl * 1024, [[128, 8], [1, 128]])
        lsrc = A(LAMQB, 0, 0, [[128, 8], [1, 128]])
        p.pool(lambda e, qsrc=qsrc, qdst=qdst, lsrc=lsrc: e.tensor_tensor(
            out=qdst, in0=qsrc, in1=lsrc, op=ALU.mult),
            reads=[bQT[pp][c // 4] for pp in range(8)] + [bLAMQ],
            writes=[bQH[csl]])

        for w in range(2):
            sa, sb = 1 + 2 * w, 2 + 2 * w
            for i in range(4):
                pp = w * 4 + i
                ob = 6 + i // 2
                ocol = (i % 2) * 256
                p.pe(lambda e, pp=pp, c=c, ob=ob, ocol=ocol, w=w, i=i: e.matmul(
                    PB[ob][:, ocol:ocol + 256],
                    VO[:, c * 1024 + pp * 128:c * 1024 + (pp + 1) * 128],
                    A(SAB, 0, w * 1024 + i * 256, [[1, 256]]),
                    start=((i % 2) == 0), stop=(c == 0 and (i % 2) == 1)),
                    reads=[bVO[c][w], bSAB[w]], writes=[bPB[ob]])
            if c > 0:
                for i in range(4):
                    pp = w * 4 + i
                    ob = 6 + i // 2
                    ocol = (i % 2) * 256
                    rhs = bass.AP(QH, csl * 1024 + pp * 128,
                                  [[2048, 128], [0, 2], [1, 128]])
                    p.pe(lambda e, pp=pp, ob=ob, ocol=ocol, rhs=rhs, csl=csl, i=i: e.matmul(
                        PB[ob][:, ocol:ocol + 256],
                        KVBD[:, csl * 1024 + pp * 128:csl * 1024 + (pp + 1) * 128],
                        rhs, start=False, stop=((i % 2) == 1)),
                        reads=[bKV[csl], bQH[csl]], writes=[bPB[ob]])
            for i in range(4):
                pp = w * 4 + i
                p.pe(lambda e, pp=pp, c=c, csl=csl, i=i: e.matmul(
                    PB[5][:, i * 128:(i + 1) * 128],
                    KTM[:, csl * 1024 + pp * 128:csl * 1024 + (pp + 1) * 128],
                    VO[:, c * 1024 + pp * 128:c * 1024 + (pp + 1) * 128],
                    start=(i == 0), stop=(i == 3)),
                    reads=[bKTM[csl][w], bVO[c][w]], writes=[bPB[5]])
            p.pool(lambda e, csl=csl, nsl=nsl, w=w: e.tensor_tensor(
                out=KVBD[:, nsl * 1024 + w * 512:nsl * 1024 + (w + 1) * 512],
                in0=KVBD[:, csl * 1024 + w * 512:csl * 1024 + (w + 1) * 512],
                in1=LAMCB[:, w * 512:(w + 1) * 512], op=ALU.mult),
                reads=[bKV[csl], bLAMC], writes=[bKV[nsl]])
            for bb_ in range(2):
                ob = 6 + bb_
                dstA = A(VO, 0, c * 1024 + (w * 4 + bb_ * 2) * 128,
                         [[128, 2], [1, 128]], rowcount=64)
                srcA = A(PB[ob], 0, 0, [[256, 2], [1, 128]], rowcount=64)
                p.act(lambda e, d=dstA, s=srcA: e.activation(
                    out=d, in_=s, func=AF.Copy),
                    reads=[bPB[ob]], writes=[bVO[c][w]])
                dstB = A(VO, 64, c * 1024 + (w * 4 + bb_ * 2) * 128,
                         [[128, 2], [1, 128]])
                srcB = A(PB[ob], 64, 128, [[256, 2], [1, 128]])
                p.act(lambda e, d=dstB, s=srcB: e.activation(
                    out=d, in_=s, func=AF.Copy),
                    reads=[bPB[ob]], writes=[bVO[c][w]])
            oA = A(KVBD, 0, nsl * 1024 + w * 512, [[128, 4], [1, 64]], rowcount=64)
            iA = A(PB[5], 0, 0, [[128, 4], [1, 64]], rowcount=64)
            p.dve(lambda e, oA=oA, iA=iA: e.tensor_tensor(
                out=oA, in0=oA, in1=iA, op=ALU.add),
                reads=[bPB[5], bKV[nsl]], writes=[bKV[nsl]])
            oB = A(KVBD, 64, nsl * 1024 + w * 512 + 64, [[128, 4], [1, 64]])
            iB = A(PB[5], 64, 64, [[128, 4], [1, 64]])
            p.dve(lambda e, oB=oB, iB=iB: e.tensor_tensor(
                out=oB, in0=oB, in1=iB, op=ALU.add),
                reads=[bPB[5], bKV[nsl]], writes=[bKV[nsl]])

        if c == 0 and "kv1" in dtaps:
            tap_toks.append(p.dma_out(
                lambda e: e.dma_start(out=dtaps["kv1"], in_=KVBD[:, 1024:2048]),
                reads=[bKV[1]]))
        if c == 0 and "ktm0" in dtaps:
            tap_toks.append(p.dma_out(
                lambda e: e.dma_start(out=dtaps["ktm0"], in_=KTM[:, 0:1024]),
                reads=bKTM[0]))
        if c == 1 and "qh1" in dtaps:
            tap_toks.append(p.dma_out(
                lambda e: e.dma_start(out=dtaps["qh1"], in_=QH[:, 1024:2048]),
                reads=[bQH[1]]))

    if "ot" in dtaps:
        tap_toks.append(p.dma_out(lambda e: e.dma_start(out=dtaps["ot"], in_=VO[:]),
                  reads=[b for c in range(NCH) for b in bVO[c]]))

    if stage < 4:
        p.final_wait(tap_toks)
        p.replay()
        return nc

    if "ws0" in dtaps:
        tap_toks.append(p.dma_out(lambda e: e.dma_start(
            out=dtaps["ws0"], in_=WS0[:]), reads=[bWS0[k] for k in range(8)]))
    for a in range(8):
        bank = a % 4
        for b_ in range(8):
            p.pe(lambda e, a=a, b_=b_, bank=bank: e.matmul(
                pbh(bank)[:, b_ * 128:(b_ + 1) * 128],
                WS1[:, b_ * 1024 + a * 128:b_ * 1024 + (a + 1) * 128], IDB[:],
                start=(b_ == 0), stop=(b_ == 7), is_transpose=True),
                reads=[bWS1[b_], bIDB], writes=[bPB[bank]])
        p.dve(lambda e, a=a, bank=bank: e.tensor_copy(
            out=WS0[:, a * 1024:(a + 1) * 1024], in_=pbh(bank)[:]),
            reads=[bPB[bank]], writes=[bWS0[a]])

    p.dma_in(lambda e: e.dma_start(out=LAND[0:100, :], in_=ddaux),
             writes=[bLAND], sem='lw5')
    p.pool(lambda e: e.tensor_copy(out=DAUXB[:], in_=LAND[0:100, :]),
           reads=[bLAND], writes=[bDAUX])
    srcu = bass.AP(duauxh, 0, [[50, 128], [6400, 8], [1, 50]])
    p.dma_in(lambda e, s=srcu: e.dma_start(out=LAND[:, 0:400], in_=s),
             writes=[bLAND], sem='lw6')
    p.pool(lambda e: e.tensor_copy(out=LSM3[:], in_=LAND[:, 0:400]),
           reads=[bLAND], writes=[bLSM])
    p.pool(lambda e: e.tensor_copy(out=LSM1[:], in_=LUPF[:]),
           reads=[bLSM], writes=[bLSM])
    p.pool(lambda e: e.tensor_copy(out=LSM2[:], in_=LDOWNF[:]),
           reads=[bLSM], writes=[bLSM])
    for gidx in range(8):
        p.pe(lambda e, g=gidx: e.matmul(
            pbh(4)[0:50, g * 128:(g + 1) * 128],
            LSM3[:, g * 50:(g + 1) * 50], IDB[:],
            start=(g == 0), stop=(g == 7), is_transpose=True),
            reads=[bLSM, bIDB], writes=[bPB[4]])
    p.dve(lambda e: e.tensor_copy(out=UAT[:], in_=pbh(4)[0:50, 0:1024]),
          reads=[bPB[4]], writes=[bUAT])
    for h in range(2):
        p.pe(lambda e, h=h: e.matmul(
            PB[4 + h][0:4, :], LSM1[:], UAT[:, h * 512:(h + 1) * 512],
            start=True, stop=True),
            reads=[bLSM, bUAT], writes=[bPB[4 + h]])
        p.act(lambda e, h=h: e.activation(
            out=AB_[0:4, h * 512:(h + 1) * 512], in_=PB[4 + h][0:4, :],
            func=AF.Copy), reads=[bPB[4 + h]], writes=[bABa])
    for h in range(2):
        p.pe(lambda e, h=h: e.matmul(
            PB[6 + h][0:100, :], LSM2[:], AB_[0:4, h * 512:(h + 1) * 512],
            start=True, stop=True),
            reads=[bLSM, bABa], writes=[bPB[6 + h]])
        p.act(lambda e, h=h: e.activation(
            out=BB_[0:100, h * 512:(h + 1) * 512], in_=PB[6 + h][0:100, :],
            func=AF.Copy), reads=[bPB[6 + h]], writes=[bABb])
    for a in range(8):
        for h in range(2):
            bank = 4 + (a * 2 + h) % 2
            p.pe(lambda e, a=a, h=h, bank=bank: e.matmul(
                PB[bank][:], DAUXB[0:100, a * 128:(a + 1) * 128],
                BB_[0:100, h * 512:(h + 1) * 512], start=True, stop=True),
                reads=[bDAUX, bABb], writes=[bPB[bank]])
            p.dve(lambda e, a=a, h=h, bank=bank: e.tensor_tensor(
                out=WS0[:, a * 1024 + h * 512:a * 1024 + (h + 1) * 512],
                in0=WS0[:, a * 1024 + h * 512:a * 1024 + (h + 1) * 512],
                in1=PB[bank][:], op=ALU.add),
                reads=[bPB[bank], bWS0[a]], writes=[bWS0[a]])

    p.dma_in(lambda e: e.dma_start(out=LAND[0:1, :], in_=dbv), writes=[bLAND], sem='lw7')
    p.pool(lambda e: e.tensor_copy(out=BVB[:], in_=LAND[0:1, :]),
           reads=[bLAND], writes=[bBV])
    for h in range(2):
        p.pe(lambda e, h=h: e.matmul(
            PB[6 + h][:], ONESB[0:1, :], BVB[0:1, h * 512:(h + 1) * 512],
            start=True, stop=True),
            reads=[bONESB, bBV], writes=[bPB[6 + h]])
        p.act(lambda e, h=h: e.activation(
            out=B128[:, h * 512:(h + 1) * 512], in_=PB[6 + h][:],
            func=AF.Copy, scale=1.0 / 128), reads=[bPB[6 + h]], writes=[bB128])

    ws.add(dwo, WS1, bWS1, extra=[bMASK])
    ws.prime()
    for _ in range(8):
        ws.cast()

    bBETB = Buf()
    p.dma_in(lambda e: e.dma_start(out=GAMB[:], in_=dgamb), writes=[bLAMC], sem='lw3')
    p.dma_in(lambda e: e.dma_start(out=BETB[:], in_=dbetb),
             writes=[bBETB, bMASK, bLANDB], sem='lw4')

    for t4 in range(4):
        for oc in range(8):
            bank = (t4 * 8 + oc) % 4
            p.pe(lambda e, t4=t4, oc=oc, bank=bank: e.matmul(
                PB[bank][:], WG2B[:, oc * 128:(oc + 1) * 128],
                T1[0:64, t4 * 512:(t4 + 1) * 512], start=True, stop=True),
                reads=[bWG2, bT1[t4]], writes=[bPB[bank]])
            p.act(lambda e, t4=t4, oc=oc, bank=bank: e.activation(
                out=GATE[:, oc * 2048 + t4 * 512:oc * 2048 + (t4 + 1) * 512],
                in_=PB[bank][:], func=AF.Sigmoid),
                reads=[bPB[bank]], writes=[bQT[oc][t4], bGATE[oc][t4]])

    for tc in range(NTILE):
        b0 = 4 + (tc % 2) * 2
        for k in range(8):
            for h in range(2):
                p.pe(lambda e, tc=tc, k=k, h=h, b0=b0: e.matmul(
                    PB[b0 + h][:],
                    XNT[:, k * 2048 + tc * 128:k * 2048 + (tc + 1) * 128],
                    WS0[:, k * 1024 + h * 512:k * 1024 + (h + 1) * 512],
                    start=(k == 0), stop=False),
                    reads=[bWS0[k], bXNT[k][tc]], writes=[bPB[b0 + h]])
        for h in range(2):
            p.pe(lambda e, tc=tc, h=h, b0=b0: e.matmul(
                PB[b0 + h][:], ONESB[:], B128[:, h * 512:(h + 1) * 512],
                start=False, stop=True),
                reads=[bONESB, bB128], writes=[bPB[b0 + h]])
        for h in range(2):
            p.act(lambda e, tc=tc, h=h, b0=b0: e.activation(
                out=G[:, tc * 1024 + h * 512:tc * 1024 + (h + 1) * 512],
                in_=PB[b0 + h][:], func=AF.Silu),
                reads=[bPB[b0 + h]],
                writes=[bKT[tc // 2][(tc % 2) * 2 + h], bG[tc]])

    if "g" in dtaps:
        tap_toks.append(p.dma_out(lambda e: e.dma_start(out=dtaps["g"], in_=G[:]),
                  reads=[bG[c] for c in range(NCH)]))
    if "gate" in dtaps:
        tap_toks.append(p.dma_out(lambda e: e.dma_start(out=dtaps["gate"], in_=GATE[:]),
                  reads=[bGATE[oc][t] for oc in range(8) for t in range(4)]))
    if "weff" in dtaps:
        tap_toks.append(p.dma_out(lambda e: e.dma_start(out=dtaps["weff"], in_=WS0[:]),
                  reads=[bWS0[k] for k in range(8)]))

    CCs = [CC, CCo]
    C2s = [CC2e, CC2o]
    bCCs = [bCC, bCC2]
    bCCs[0].w = list(bLAMQ.w) + list(bLAMQ.r)
    bCCs[1].w = list(bLAMK.w) + list(bLAMK.r)
    bSG2.w = list(bSAB[0].w) + list(bSAB[0].r) + list(bSAB[1].w) + list(bSAB[1].r)
    bC2s = [Buf(), Buf()]
    bC2s[0].w = (list(bMASK.w) + list(bMASK.r)
                 + list(bLANDB.w) + list(bLANDB.r))
    bC2s[1].w = (list(bKTM[0][0].w) + list(bKTM[0][0].r)
                 + list(bKTM[0][1].w) + list(bKTM[0][1].r)
                 + list(bBV.w) + list(bBV.r))
    bCCP = [[Buf() for _ in range(8)] for _ in range(2)]
    for par in range(2):
        for pp in range(8):
            bCCP[par][pp].w = list(bCCs[par].w)
    out_toks = []

    def p4c_A(tc):
        par = tc % 2
        sb_st = [0, 2][par]
        sb_b = [1, 3][par]
        CCt, C2t = CCs[par], C2s[par]
        bCCt, bC2t = bCCs[par], bC2s[par]
        for pp in range(8):
            p.pe(lambda e, tc=tc, pp=pp, sb_st=sb_st: e.matmul(
                PB[sb_st][0:1, 0:128], ONESB[:, 0:1],
                VO[:, tc * 1024 + pp * 128:tc * 1024 + (pp + 1) * 128],
                start=(pp == 0), stop=(pp == 7)),
                reads=[bONESB, bVO[tc][pp // 4]], writes=[bPB[sb_st]])
        p.act(lambda e, sb_st=sb_st: e.activation(
            out=MROW[:], in_=PB[sb_st][0:1, 0:128], func=AF.Copy, scale=1.0 / E),
            reads=[bPB[sb_st]], writes=[bROW])
        p.pe(lambda e, sb_b=sb_b: e.matmul(
            PB[sb_b][:, 0:128], ONESF[:], MROW[:], start=True, stop=True),
            reads=[bONESF, bROW], writes=[bPB[sb_b]])
        p.dve(lambda e, tc=tc, CCt=CCt, sb_b=sb_b: e.tensor_tensor(
            out=A(CCt, 0, 0, [[128, 8], [1, 128]]),
            in0=A(VO, 0, tc * 1024, [[128, 8], [1, 128]]),
            in1=A(PB[sb_b], 0, 0, [[0, 8], [1, 128]]),
            op=ALU.subtract),
            reads=bVO[tc] + [bPB[sb_b]], writes=[bCCt] + bCCP[par])
        p.act(lambda e, CCt=CCt, C2t=C2t: e.activation(
            out=C2t[:], in_=CCt[:], func=AF.Square),
            reads=[bCCt] + bCCP[par], writes=[bC2t])

    def p4c_B(tc):
        par = tc % 2
        sb_st = [0, 2][par]
        sb_b = [1, 3][par]
        CCt, C2t = CCs[par], C2s[par]
        bCCt, bC2t = bCCs[par], bC2s[par]
        for pp in range(8):
            p.pe(lambda e, pp=pp, C2t=C2t, sb_st=sb_st: e.matmul(
                PB[sb_st][0:1, 128:256], ONESB[:, 0:1],
                C2t[:, pp * 128:(pp + 1) * 128],
                start=(pp == 0), stop=(pp == 7)),
                reads=[bONESB, bC2t], writes=[bPB[sb_st]])
        p.act(lambda e, sb_st=sb_st: e.activation(
            out=VROW[:], in_=PB[sb_st][0:1, 128:256],
            func=AF.Sqrt, scale=1.0 / E, bias=EPS),
            reads=[bPB[sb_st], bEPS], writes=[bROW])
        p.pe(lambda e, sb_b=sb_b: e.matmul(
            PB[sb_b][:, 128:256], ONESF[:], VROW[:], start=True, stop=True),
            reads=[bONESF, bROW], writes=[bPB[sb_b]])
        p.dve(lambda e, sb_b=sb_b: e.reciprocal(
            out=RBI[:], in_=PB[sb_b][:, 128:256]),
            reads=[bPB[sb_b]], writes=[bB128])
        for pp in range(8):
            bP = bCCP[par][pp]
            p.dve(lambda e, CCt=CCt, pp=pp: e.scalar_tensor_tensor(
                out=CCt[:, pp * 128:(pp + 1) * 128],
                in0=CCt[:, pp * 128:(pp + 1) * 128],
                scalar=GAMB[:, pp * 128:pp * 128 + 1],
                in1=RBI[:], op0=ALU.mult, op1=ALU.mult),
                reads=[bP, bB128, bLAMC], writes=[bP])
            p.dve(lambda e, tc=tc, CCt=CCt, pp=pp: e.scalar_tensor_tensor(
                out=CCt[:, pp * 128:(pp + 1) * 128],
                in0=CCt[:, pp * 128:(pp + 1) * 128],
                scalar=BETB[:, pp * 128:pp * 128 + 1],
                in1=GATE[:, pp * 2048 + tc * 128:pp * 2048 + (tc + 1) * 128],
                op0=ALU.add, op1=ALU.mult),
                reads=[bP, bBETB, bGATE[pp][tc // 4]], writes=[bP])

    def p4c_S(tc):
        par = tc % 2
        sb0 = 4 + par * 2
        CCt, bCCt = CCs[par], bCCs[par]
        for ec in range(8):
            for h in range(2):
                p.pe(lambda e, ec=ec, h=h, sb0=sb0, CCt=CCt: e.matmul(
                    PB[sb0 + h][:], CCt[:, ec * 128:(ec + 1) * 128],
                    WS1[:, ec * 1024 + h * 512:ec * 1024 + (h + 1) * 512],
                    start=(ec == 0), stop=(ec == 7)),
                    reads=[bCCP[par][ec], bWS1[ec]], writes=[bPB[sb0 + h]])
        sgt, sgb = (SGT, bSG) if tc % 2 == 0 else (SGT2, bSG2)
        for h in range(2):
            p.dve(lambda e, tc=tc, h=h, sb0=sb0, sgt=sgt: e.tensor_tensor(
                out=sgt[:, h * 512:(h + 1) * 512], in0=PB[sb0 + h][:],
                in1=G[:, tc * 1024 + h * 512:tc * 1024 + (h + 1) * 512],
                op=ALU.mult),
                reads=[bPB[sb0 + h], bG[tc]], writes=[sgb])
        tok = p.dma_out(lambda e, tc=tc, sgt=sgt: e.dma_start(
            out=dy[tc * 128:(tc + 1) * 128, :], in_=sgt[:],
            accum_op=ALU.add), reads=[sgb, bDY])
        out_toks.append(tok)

    p4c_A(0)
    p4c_B(0)
    for tc in range(1, NTILE):
        p4c_A(tc)
        p4c_S(tc - 1)
        p4c_B(tc)
    p4c_S(NTILE - 1)

    p.final_wait(out_toks)
    p.replay()
    return nc



def _bcast_affine(v):
    g = np.asarray(v, dtype=np.float32).reshape(8, 128).T.astype(BF)
    return np.ascontiguousarray(
        np.broadcast_to(g[:, :, None], (128, 8, 128)).reshape(128, 1024))


_NC_CACHE = {}


def kernel(L0_attn_x, L0_attn_Wq, L0_attn_Wk, L0_attn_Wv, L0_attn_Wo,
           L0_attn_gamma, L0_attn_beta, L0_attn_Wg1, L0_attn_Wg2,
           L0_ffn_W, L0_ffn_b, L0_ffn_lora_down, L0_ffn_lora_up,
           L0_ffn_down_aux, L0_ffn_up_aux):
    taps = tuple(DEBUG_TAPS)
    key = (taps, STOP_STAGE)
    if key not in _NC_CACHE:
        _NC_CACHE[key] = build(taps, STOP_STAGE)
    nc = _NC_CACHE[key]

    cst = _consts()
    f32 = np.float32
    shared = {
        "wq": np.ascontiguousarray(L0_attn_Wq, dtype=f32),
        "wk": np.ascontiguousarray(L0_attn_Wk, dtype=f32),
        "wv": np.ascontiguousarray(L0_attn_Wv, dtype=f32),
        "wo": np.ascontiguousarray(L0_attn_Wo, dtype=f32),
        "w": np.ascontiguousarray(L0_ffn_W, dtype=f32),
        "wg1": np.ascontiguousarray(L0_attn_Wg1, dtype=f32),
        "wg2": np.ascontiguousarray(L0_attn_Wg2, dtype=f32),
        "gamb": _bcast_affine(L0_attn_gamma),
        "betb": _bcast_affine(L0_attn_beta),
        "bvec": np.ascontiguousarray(np.asarray(L0_ffn_b, dtype=f32).reshape(1, E)),
        "ldown": np.ascontiguousarray(L0_ffn_lora_down, dtype=f32),
        "lup": np.ascontiguousarray(L0_ffn_lora_up, dtype=f32),
        "daux": np.ascontiguousarray(L0_ffn_down_aux, dtype=f32),
        "uaux": np.ascontiguousarray(L0_ffn_up_aux, dtype=f32),
        "maskt": cst["maskt"], "lamqb": cst["lamqb"], "lamkb": cst["lamkb"],
        "lamcb": cst["lamcb"], "identb": cst["identb"],
        "onesb": cst["onesb"], "onesf": cst["onesf"],
    }
    xs = np.ascontiguousarray(np.asarray(L0_attn_x, dtype=f32))
    in_maps = [dict(shared, x=xs[b]) for b in range(8)]
    res = run_bass_kernel_spmd(nc, in_maps, core_ids=list(range(8)))
    out = np.stack([np.asarray(res.results[b]["y"], dtype=f32) for b in range(8)])
    if taps:
        kernel.last_results = res
    return out
